# version 05
# baseline (speedup 1.0000x reference)
import os
import contextlib
import numpy as np

import concourse.bass as bass
import concourse.mybir as mybir
from concourse.bass_utils import run_bass_kernel_spmd

F32 = mybir.dt.float32
BF16 = mybir.dt.bfloat16
I32 = mybir.dt.int32
AF = mybir.ActivationFunctionType
ALU = mybir.AluOpType
AXX = mybir.AxisListType.X

NCORES = 8
D = 4096
DC = 32
S = 2048
NT = 4
TW = 512
KD = 1024
KC = 8
NH = 32
RK = 16
EPS_LN = 1e-5
EPS_ATT = 1e-10

STAGE = int(os.environ.get("ATHENA_STAGE", "3"))
SUB = os.environ.get("ATHENA_SUB", "full")

ENGINES = ("tensor", "vector", "scalar", "sync", "gpsimd")


class Sem:

    def __init__(self, es, nc, name):
        self.h = es.enter_context(nc.semaphore(name))
        self.name = name
        self.n = 0
        self.bounds = {0}

    def inc(self, by):
        assert not getattr(self, "need_mark", False), \
            (self.name, "dma issued after an unboundaried wait resolved to total", self.n)
        self.n += by
        return self.n

    def mark(self):
        self.bounds.add(self.n)
        self.need_mark = False

    def wait_val(self, v):
        cands = [x for x in self.bounds if x >= v]
        if not cands and self.n >= v:
            self.need_mark = True
            return self.n
        assert cands, ("no boundary for wait", self.name, v, self.n)
        return min(cands)


class B:
    def __init__(self, nc, es):
        self.nc = nc
        self.ops = {e: [] for e in ENGINES}
        self.pe = Sem(es, nc, "pe")
        self.dve = Sem(es, nc, "dve")
        self.act = Sem(es, nc, "act")
        self.cc = Sem(es, nc, "cc")
        self.w0 = Sem(es, nc, "w0")
        self.w1 = Sem(es, nc, "w1")
        self.gq = Sem(es, nc, "gq")
        self.sx = Sem(es, nc, "sx")
        self.so = Sem(es, nc, "so")
        self.lq = [Sem(es, nc, f"lq{i}") for i in range(8)]
        self._dma = {"gpsimd": [self.w0, self.w1, self.gq] + self.lq,
                     "sync": [self.sx, self.so]}
        self._is_dma = {self.w0, self.w1, self.gq, self.sx, self.so, *self.lq}

    def wait(self, eng, sem, v):
        val = sem.wait_val(v) if sem in self._is_dma else v
        assert val <= sem.n, (eng, sem.name, v, sem.n)
        self.ops[eng].append(lambda e, _s=sem.h, _v=val: e.wait_ge(_s, _v))
        for s in self._dma.get(eng, ()):
            s.mark()

    def dma(self, eng, sem, out_ap, in_ap):
        self.ops[eng].append(
            lambda e, _o=out_ap, _i=in_ap, _s=sem.h: e.dma_start(out=_o, in_=_i).then_inc(_s, 16))
        return sem.inc(16)

    def op(self, eng, fn, inc=None):
        if inc is None:
            self.ops[eng].append(fn)
            return None
        self.ops[eng].append(lambda e, _f=fn, _s=inc.h: _f(e).then_inc(_s, 1))
        v = inc.inc(1)
        inc.mark()
        return v


def _emit_all(block, b):
    @block.tensor
    def _(e):
        for f in b.ops["tensor"]:
            f(e)

    @block.vector
    def _(e):
        for f in b.ops["vector"]:
            f(e)

    @block.scalar
    def _(e):
        for f in b.ops["scalar"]:
            f(e)

    @block.sync
    def _(e):
        for f in b.ops["sync"]:
            f(e)

    @block.gpsimd
    def _(e):
        for f in b.ops["gpsimd"]:
            f(e)


def build_nc():
    nc = bass.Bass("TRN2", target_bir_lowering=False)
    es = contextlib.ExitStack()
    ap = lambda t: t.ap()

    xT = ap(nc.dram_tensor("xT", [D, S], F32, kind="ExternalInput"))
    wq_t = ap(nc.dram_tensor("wq_t", [DC, 128, DC, 128], F32, kind="ExternalInput"))
    wo_t = ap(nc.dram_tensor("wo_t", [DC, 128, DC, 128], F32, kind="ExternalInput"))
    wf_t = ap(nc.dram_tensor("wf_t", [DC, 128, DC, 128], F32, kind="ExternalInput"))
    wk_t = ap(nc.dram_tensor("wk_t", [KC, 128, DC, 128], F32, kind="ExternalInput"))
    wv_t = ap(nc.dram_tensor("wv_t", [KC, 128, DC, 128], F32, kind="ExternalInput"))
    bq_h = ap(nc.dram_tensor("bq_h", [128, DC], F32, kind="ExternalInput"))
    bo_h = ap(nc.dram_tensor("bo_h", [128, DC], F32, kind="ExternalInput"))
    bf_h = ap(nc.dram_tensor("bf_h", [128, DC], F32, kind="ExternalInput"))
    bk_h = ap(nc.dram_tensor("bk_h", [128, KC], F32, kind="ExternalInput"))
    bv_h = ap(nc.dram_tensor("bv_h", [128, KC], F32, kind="ExternalInput"))
    la_h = ap(nc.dram_tensor("la_h", [128, 3, DC, RK], F32, kind="ExternalInput"))
    lb_h = ap(nc.dram_tensor("lb_h", [RK, 3, DC, 128], F32, kind="ExternalInput"))
    lsi_h = ap(nc.dram_tensor("lsi_h", [1, 4], I32, kind="ExternalInput"))
    iota_h = ap(nc.dram_tensor("iota_h", [1, 3], F32, kind="ExternalInput"))
    ident_h = ap(nc.dram_tensor("ident_h", [128, 128], F32, kind="ExternalInput"))

    out = ap(nc.dram_tensor("out", [D, S], F32, kind="ExternalOutput"))

    x1_s = ap(nc.dram_tensor("x1_s", [DC, 128, S], F32))
    x2_s = ap(nc.dram_tensor("x2_s", [DC, 128, S], F32))
    q_s = ap(nc.dram_tensor("q_s", [NH, 128, S], BF16))
    k_s = ap(nc.dram_tensor("k_s", [KC, 128, S], BF16))
    v_s = ap(nc.dram_tensor("v_s", [KC, 128, S], BF16))
    cc_in = ap(nc.dram_tensor("cc_in", [128, 1032], F32))
    cc_out = ap(nc.dram_tensor("cc_out", [128, 1032], F32))

    sb = lambda n, shp, dt: es.enter_context(nc.sbuf_tensor(n, shp, dt))
    h_buf = sb("h_buf", [128, DC * S], BF16)
    wslab = [sb(f"wslab{i}", [128, DC * 128], BF16) for i in range(2)]
    a_bc = sb("a_bc", [128, S], BF16)
    nm_bc = sb("nm_bc", [128, S], BF16)
    rows = sb("rows", [1, 2048], F32)
    qkvst = [sb(f"qkvst{i}", [128, S], BF16) for i in range(2)]
    u_sb = sb("u_sb", [RK, S], BF16)
    a_eff = sb("a_eff", [128, DC * RK], BF16)
    b_eff = sb("b_eff", [RK, D], BF16)
    bq_sb = sb("bq_sb", [128, DC], F32)
    bo_sb = sb("bo_sb", [128, DC], F32)
    bf_sb = sb("bf_sb", [128, DC], F32)
    bfh_sb = sb("bfh_sb", [128, DC], F32)
    bk_sb = sb("bk_sb", [128, KC], F32)
    bv_sb = sb("bv_sb", [128, KC], F32)
    nbq = sb("nbq", [128, DC], F32)
    bq1 = sb("bq1", [128, DC], F32)
    nbk = sb("nbk", [128, KC], F32)
    bk1 = sb("bk1", [128, KC], F32)
    kvt_sb = sb("kvt_sb", [128, KD], F32)
    kvtn = sb("kvtn", [128, KD], BF16)
    ksum_sb = sb("ksum_sb", [128, KC], F32)
    rn_a = sb("rn_a", [128, KC], F32)
    rn_b = sb("rn_b", [128, KC], F32)
    kT_t = [sb(f"kT{i}", [128, 128], BF16) for i in range(2)]
    vT_t = [sb(f"vT{i}", [128, 128], BF16) for i in range(2)]
    ident_sb = sb("ident_sb", [128, 128], BF16)
    ones_c = sb("ones_c", [128, 1], BF16)
    ones_r = sb("ones_r", [1, 128], F32)
    iota_sb = sb("iota_sb", [1, 3], F32)
    lsi_sb = sb("lsi_sb", [1, 4], I32)
    lsw = sb("lsw", [1, 20], F32)
    m_row = sb("m_row", [1, 3], F32)
    m_sb = sb("m_sb", [128, 3], F32)

    es_setup = contextlib.ExitStack()
    la_st = es_setup.enter_context(nc.sbuf_tensor("la_st", [128, 3 * DC * RK], BF16))
    lb_st = es_setup.enter_context(nc.sbuf_tensor("lb_st", [RK, S], BF16))
    tmp_a = es_setup.enter_context(nc.sbuf_tensor("tmp_a", [128, DC * RK], BF16))
    tmp_b = es_setup.enter_context(nc.sbuf_tensor("tmp_b", [RK, S], BF16))

    psA = [es.enter_context(nc.psum_tensor(f"ps{i}", [128, 512], F32)) for i in range(8)]

    b = B(nc, es)

    b.dma("sync", b.sx, bq_sb[:, :], bq_h)
    b.dma("sync", b.sx, bo_sb[:, :], bo_h)
    b.dma("sync", b.sx, bf_sb[:, :], bf_h)
    b.dma("sync", b.sx, bk_sb[:, :], bk_h)
    b.dma("sync", b.sx, bv_sb[:, :], bv_h)
    b.dma("sync", b.sx, lsi_sb[:, :], lsi_h)
    setup_sx = b.dma("sync", b.sx, iota_sb[:, :], iota_h)
    b.dma("gpsimd", b.gq, ident_sb[:, :], ident_h)
    setup_la = b.dma("gpsimd", b.gq,
                     la_st[:, :].rearrange("p (i c r) -> p i c r", i=3, c=DC), la_h)

    b.op("vector", lambda e: e.memset(ones_c[:, :], 1.0))
    b.op("vector", lambda e: e.memset(ones_r[:, :], 1.0))
    b.wait("vector", b.sx, setup_sx)
    b.op("vector", lambda e: e.tensor_scalar(nbq[:, :], bq_sb[:, :], -1.0, None, ALU.mult))
    b.op("vector", lambda e: e.tensor_scalar(bq1[:, :], bq_sb[:, :], 1.0, None, ALU.add))
    b.op("vector", lambda e: e.tensor_scalar(bfh_sb[:, :], bf_sb[:, :], 0.5, None, ALU.mult))
    b.op("vector", lambda e: e.tensor_scalar(nbk[:, :], bk_sb[:, :], -1.0, None, ALU.mult))
    b.op("vector", lambda e: e.tensor_scalar(bk1[:, :], bk_sb[:, :], 1.0, None, ALU.add))
    L = lsw
    _vc = [None]

    def VC(fn):
        if _vc[0] is not None:
            b.wait("vector", b.dve, _vc[0])
        _vc[0] = b.op("vector", fn, inc=b.dve)
        return _vc[0]

    VC(lambda e: e.tensor_scalar(L[0:1, 0:4], lsi_sb[:, :], 0.0, None, ALU.is_le))
    VC(lambda e: e.tensor_scalar(L[0:1, 4:5], L[0:1, 3:4], 3.0, None, ALU.add))
    VC(lambda e: e.tensor_scalar(L[0:1, 5:6], L[0:1, 1:2], 2.0, None, ALU.mult))
    VC(lambda e: e.tensor_scalar(L[0:1, 6:7], L[0:1, 1:2], -1.0, 1.0, ALU.mult, ALU.add))
    VC(lambda e: e.tensor_mul(L[0:1, 7:8], L[0:1, 6:7], L[0:1, 0:1]))
    VC(lambda e: e.tensor_add(L[0:1, 8:9], L[0:1, 5:6], L[0:1, 7:8]))
    VC(lambda e: e.tensor_mul(L[0:1, 9:10], L[0:1, 2:3], L[0:1, 4:5]))
    VC(lambda e: e.tensor_scalar(L[0:1, 10:11], L[0:1, 2:3], -1.0, 1.0, ALU.mult, ALU.add))
    VC(lambda e: e.tensor_mul(L[0:1, 11:12], L[0:1, 10:11], L[0:1, 8:9]))
    VC(lambda e: e.tensor_add(L[0:1, 12:13], L[0:1, 9:10], L[0:1, 11:12]))
    VC(lambda e: e.tensor_scalar(L[0:1, 13:14], L[0:1, 12:13], -1.0, None, ALU.add))
    VC(lambda e: e.tensor_scalar(L[0:1, 14:15], L[0:1, 13:14], 1.0, None, ALU.is_ge))
    VC(lambda e: e.tensor_scalar(L[0:1, 15:16], L[0:1, 13:14], -1.0, 0.0, ALU.add, ALU.max))
    VC(lambda e: e.tensor_scalar(L[0:1, 16:17], L[0:1, 15:16], 2.0, None, ALU.min))
    VC(lambda e: e.tensor_scalar(L[0:1, 17:20], iota_sb[:, :], L[0:1, 16:17], None, ALU.is_equal))
    mask_done = VC(lambda e: e.tensor_scalar(m_row[:, :], L[0:1, 17:20], L[0:1, 14:15], None, ALU.mult))
    b.wait("tensor", b.dve, mask_done)
    mbc = b.op("tensor", lambda e: e.matmul(psA[0][:, 0:3], ones_r[:, :], m_row[:, :],
                                            start=True, stop=True), inc=b.pe)
    b.wait("vector", b.pe, mbc)
    msb_done = b.op("vector", lambda e: e.tensor_copy(m_sb[:, :], psA[0][:, 0:3]), inc=b.dve)
    b.wait("gpsimd", b.dve, msb_done)

    b.wait("vector", b.gq, setup_la)
    b.wait("vector", b.dve, msb_done)
    _a1 = b.op("vector", lambda e: e.tensor_scalar(a_eff[:, :], la_st[:, 0:DC * RK],
                                                   m_sb[:, 0:1], None, ALU.mult), inc=b.dve)
    b.wait("vector", b.dve, _a1)
    _a2 = b.op("vector", lambda e: e.scalar_tensor_tensor(tmp_a[:, :], la_st[:, DC * RK:2 * DC * RK],
                                                          m_sb[:, 1:2], a_eff[:, :], ALU.mult, ALU.add),
               inc=b.dve)
    b.wait("vector", b.dve, _a2)
    aeff_last = b.op("vector",
                     lambda e: e.scalar_tensor_tensor(a_eff[:, :], la_st[:, 2 * DC * RK:3 * DC * RK],
                                                      m_sb[:, 2:3], tmp_a[:, :], ALU.mult, ALU.add),
                     inc=b.dve)
    b.wait("gpsimd", b.dve, aeff_last)
    done = None
    for hh in range(2):
        cs = slice(hh * 2048, (hh + 1) * 2048)
        for i in range(3):
            ld = b.dma("gpsimd", b.gq,
                       lb_st[:, :].rearrange("r (c o) -> r c o", c=16),
                       lb_h[:, i, hh * 16:(hh + 1) * 16, :])
            b.wait("vector", b.gq, ld)
            if i > 0 or hh > 0:
                b.wait("vector", b.dve, done)
            if i == 0:
                done = b.op("vector", lambda e, _c=cs: e.tensor_scalar(
                    b_eff[:, _c], lb_st[:, :], m_sb[0:RK, 0:1], None, ALU.mult), inc=b.dve)
            elif i == 1:
                done = b.op("vector", lambda e, _c=cs: e.scalar_tensor_tensor(
                    tmp_b[:, :], lb_st[:, :], m_sb[0:RK, 1:2], b_eff[:, _c], ALU.mult, ALU.add),
                    inc=b.dve)
            else:
                done = b.op("vector", lambda e, _c=cs: e.scalar_tensor_tensor(
                    b_eff[:, _c], lb_st[:, :], m_sb[0:RK, 2:3], tmp_b[:, :], ALU.mult, ALU.add),
                    inc=b.dve)
            b.wait("gpsimd", b.dve, done)

    setup_gq_total = b.gq.n
    setup_dve_total = b.dve.n
    es_setup.close()
    xres = [es.enter_context(nc.sbuf_tensor(f"xres{i}", [128, TW], F32)) for i in range(2)]
    outst = [es.enter_context(nc.sbuf_tensor(f"outst{i}", [128, TW], F32)) for i in range(2)]
    xsq = [es.enter_context(nc.sbuf_tensor(f"xsq{i}", [128, 1024], BF16)) for i in range(2)]
    tmp_f = es.enter_context(nc.sbuf_tensor("tmp_f", [128, TW], F32))
    tmp_g = es.enter_context(nc.sbuf_tensor("tmp_g", [128, TW], F32))

    def h_slice(dc, half):
        return h_buf[:, dc * S + half * 1024: dc * S + (half + 1) * 1024]

    def ln_phase(src_full):
        pe_entry = b.pe.n
        b.wait("gpsimd", b.pe, pe_entry)
        b.wait("gpsimd", b.act, b.act.n)
        order = [(dc, half) for half in range(2) for dc in range(DC)]
        ld = {}
        for dcp in range(8):
            ld[dcp] = b.dma("gpsimd", b.lq[dcp % 8],
                            h_buf[:, dcp * S:(dcp + 1) * S], src_full(dcp))
        mm_done = {}
        cp_dve = {}
        bsum = (psA[4], psA[6])
        bsq = (psA[5], psA[7])
        for ki, (dc, half) in enumerate(order):
            p = ki % 2
            b.wait("scalar", b.lq[dc % 8], ld[dc])
            if ki >= 2:
                b.wait("scalar", b.pe, mm_done[order[ki - 2]])
            else:
                b.wait("scalar", b.pe, pe_entry)
            sq = b.op("scalar", lambda e, _p=p, _s=h_slice(dc, half): e.activation(
                xsq[_p][:, :], _s, AF.Square), inc=b.act)
            b.wait("tensor", b.lq[dc % 8], ld[dc])
            b.wait("tensor", b.act, sq)
            st, sp_ = (dc == 0), (dc == DC - 1)
            if ki == 0:
                b.wait("tensor", b.dve, b.dve.n)
            elif half == 1 and dc == 0:
                b.wait("tensor", b.dve, cp_dve[0])
            b.op("tensor", lambda e, _st=st, _sp=sp_, _o=bsum[0], _s=h_slice(dc, half): e.matmul(
                _o[0:1, :], ones_c[:, :], _s[:, 0:512], start=_st, stop=_sp))
            b.op("tensor", lambda e, _p=p, _st=st, _sp=sp_, _o=bsq[0]: e.matmul(
                _o[0:1, :], ones_c[:, :], xsq[_p][:, 0:512], start=_st, stop=_sp))
            b.op("tensor", lambda e, _st=st, _sp=sp_, _o=bsum[1], _s=h_slice(dc, half): e.matmul(
                _o[0:1, :], ones_c[:, :], _s[:, 512:1024], start=_st, stop=_sp))
            mm_done[(dc, half)] = b.op("tensor", lambda e, _p=p, _st=st, _sp=sp_, _o=bsq[1]: e.matmul(
                _o[0:1, :], ones_c[:, :], xsq[_p][:, 512:1024], start=_st, stop=_sp), inc=b.pe)
            if half == 0 and dc + 8 < DC:
                b.wait("gpsimd", b.pe, mm_done[(dc, half)])
                ld[dc + 8] = b.dma("gpsimd", b.lq[(dc + 8) % 8],
                                   h_buf[:, (dc + 8) * S:(dc + 9) * S], src_full(dc + 8))
            if dc != DC - 1:
                continue
            stats_pe = b.pe.n
            for j in range(2):
                S_ps, Q_ps = bsum[j], bsq[j]
                q0 = rows[0:1, 0:512]
                q1 = rows[0:1, 512:1024]
                q2 = rows[0:1, 1024:1536]
                q3 = rows[0:1, 1536:2048]
                b.wait("vector", b.pe, stats_pe if j == 0 else b.pe.n)
                c0 = b.op("vector", lambda e, _S=S_ps, _q=q0: e.tensor_scalar(
                    _q, _S[0:1, :], -1.0 / D, None, ALU.mult), inc=b.dve)
                c1 = b.op("vector", lambda e, _Q=Q_ps, _q=q1: e.tensor_scalar(
                    _q, _Q[0:1, :], 1.0 / D, EPS_LN, ALU.mult, ALU.add), inc=b.dve)
                b.wait("vector", b.dve, c1)
                v0 = b.op("vector", lambda e, _q0=q0, _q2=q2: e.tensor_mul(_q2, _q0, _q0),
                          inc=b.dve)
                b.wait("vector", b.dve, v0)
                v1 = b.op("vector", lambda e, _q1=q1, _q2=q2, _q3=q3: e.tensor_sub(_q3, _q1, _q2),
                          inc=b.dve)
                b.wait("scalar", b.dve, v1)
                a0 = b.op("scalar", lambda e, _q2=q2, _q3=q3: e.activation(_q2, _q3, AF.Ln),
                          inc=b.act)
                b.wait("scalar", b.act, a0)
                a1 = b.op("scalar", lambda e, _q1=q1, _q2=q2: e.activation(
                    _q1, _q2, AF.Exp, bias=0.0, scale=-0.5), inc=b.act)
                b.wait("tensor", b.dve, v1)
                b.wait("tensor", b.act, a1)
                b.op("tensor", lambda e, _S=S_ps, _q=q1: e.matmul(
                    _S[:, :], ones_r[:, :], _q, start=True, stop=True))
                bc = b.op("tensor", lambda e, _Q=Q_ps, _q=q0: e.matmul(
                    _Q[:, :], ones_r[:, :], _q, start=True, stop=True), inc=b.pe)
                b.wait("vector", b.pe, bc)
                cls = slice(half * 1024 + j * 512, half * 1024 + (j + 1) * 512)
                b.op("vector", lambda e, _S=S_ps, _c=cls: e.tensor_copy(a_bc[:, _c], _S[:, :]))
                b.op("vector", lambda e, _Q=Q_ps, _c=cls: e.tensor_copy(nm_bc[:, _c], _Q[:, :]),
                     inc=b.dve)
            cp_dve[half] = b.dve.n
            b.wait("vector", b.dve, cp_dve[half])
            nms = nm_bc[:, half * 1024:(half + 1) * 1024]
            for sdc in range(DC):
                b.op("vector", lambda e, _h=h_slice(sdc, half), _n=nms: e.tensor_tensor(
                    _h, _h, _n, ALU.add), inc=(b.dve if sdc == DC - 1 else None))

    def lora_u_phase():
        b.wait("tensor", b.dve, b.dve.n)
        b.wait("tensor", b.act, b.act.n)
        mm = None
        for dc in range(DC):
            for tt in range(NT):
                mm = b.op("tensor", lambda e, _dc=dc, _t=tt, _st=(dc == 0), _sp=(dc == DC - 1):
                          e.matmul(psA[4 + _t][0:RK, :], a_eff[:, _dc * RK:(_dc + 1) * RK],
                                   h_buf[:, _dc * S + _t * TW:_dc * S + (_t + 1) * TW],
                                   start=_st, stop=_sp),
                          inc=(b.pe if (dc == DC - 1 and tt == NT - 1) else None))
        b.wait("vector", b.pe, mm)
        uv = None
        for tt in range(NT):
            uv = b.op("vector", lambda e, _t=tt: e.tensor_copy(
                u_sb[:, _t * TW:(_t + 1) * TW], psA[4 + _t][0:RK, :]), inc=b.dve)
        return uv

    def mm_sweep(kind, n_oc, slab_src, dst_tile=None, res_tile=None, store_chunk=None,
                 bias=None, scale=None, lora=False, res_ready=0):
        slab_done = {}
        wsem = (b.w0, b.w1)
        b.wait("gpsimd", b.pe, b.pe.n)
        for oc in range(min(2, n_oc)):
            slab_done[oc] = b.dma("gpsimd", wsem[oc % 2],
                                  wslab[oc % 2][:, :].rearrange("p (c o) -> p c o", c=DC),
                                  slab_src(oc))
        grp = {}
        rel_dve = {}
        rel_act = {}
        prev_a1 = {}
        prev_a2 = None
        dve_entry = b.dve.n
        xres_ld = {}
        v2_done = {}
        store_done = {}
        if kind == "ffn":
            if res_ready:
                b.wait("sync", b.so, res_ready)
            b.wait("sync", b.pe, b.pe.n)
            b.wait("sync", b.gq, setup_gq_total)
            b.wait("sync", b.dve, setup_dve_total)
            for kk in range(2):
                xres_ld[kk] = b.dma("sync", b.sx, xres[kk % 2][:, :], res_tile(0, kk))
        for oc in range(n_oc):
            base = 4 * (oc % 2)
            b.wait("tensor", wsem[oc % 2], slab_done[oc])
            if oc == 0:
                b.wait("tensor", b.dve, b.dve.n)
                b.wait("tensor", b.act, b.act.n)
                b.wait("vector", b.dve, dve_entry)
            if oc >= 2:
                b.wait("tensor", b.dve, rel_dve[oc - 2])
                if (oc - 2) in rel_act:
                    b.wait("tensor", b.act, rel_act[oc - 2])
            for dc in range(DC):
                for tt in range(NT):
                    sp_ = (dc == DC - 1) and not lora
                    last = (not lora) and (dc == DC - 1) and (tt == NT - 1)
                    grp[oc] = b.op("tensor",
                                   lambda e, _oc=oc, _dc=dc, _t=tt, _st=(dc == 0), _sp=sp_, _b=base:
                                   e.matmul(psA[_b + _t][:, :],
                                            wslab[_oc % 2][:, _dc * 128:(_dc + 1) * 128],
                                            h_buf[:, _dc * S + _t * TW:_dc * S + (_t + 1) * TW],
                                            start=_st, stop=_sp),
                                   inc=(b.pe if last else None))
            if lora:
                for tt in range(NT):
                    grp[oc] = b.op("tensor", lambda e, _oc=oc, _t=tt, _b=base: e.matmul(
                        psA[_b + _t][:, :], b_eff[:, _oc * 128:(_oc + 1) * 128],
                        u_sb[:, _t * TW:(_t + 1) * TW], start=False, stop=True),
                        inc=(b.pe if tt == NT - 1 else None))
            if oc + 2 < n_oc:
                b.wait("gpsimd", b.pe, grp[oc])
                slab_done[oc + 2] = b.dma("gpsimd", wsem[oc % 2],
                                          wslab[oc % 2][:, :].rearrange("p (c o) -> p c o", c=DC),
                                          slab_src(oc + 2))
            if kind == "ffn":
                for tt in range(NT):
                    k = oc * NT + tt
                    p = k % 2
                    b.wait("vector", b.pe, grp[oc])
                    if k >= 1:
                        b.wait("vector", b.so, store_done[k - 1])
                    if scale is not None:
                        r1 = b.op("vector", lambda e, _oc=oc, _t=tt, _b=base:
                                  e.scalar_tensor_tensor(
                                      tmp_f[:, :], psA[_b + _t][:, :], scale,
                                      a_bc[:, _t * TW:(_t + 1) * TW], ALU.mult, ALU.mult),
                                  inc=b.dve)
                    else:
                        r1 = b.op("vector", lambda e, _oc=oc, _t=tt, _b=base: e.tensor_scalar(
                            tmp_f[:, :], psA[_b + _t][:, :], bias[:, _oc:_oc + 1], None,
                            ALU.add), inc=b.dve)
                    b.wait("vector", b.sx, xres_ld[min(k + 1, n_oc * NT - 1)])
                    b.wait("vector", b.dve, r1)
                    if scale is not None:
                        v2_done[k] = b.op("vector", lambda e, _p=p, _oc=oc:
                                          e.scalar_tensor_tensor(
                                              outst[_p][:, :], xres[_p][:, :],
                                              bfh_sb[:, _oc:_oc + 1], tmp_f[:, :],
                                              ALU.add, ALU.add), inc=b.dve)
                    else:
                        v2_done[k] = b.op("vector", lambda e, _p=p: e.tensor_add(
                            outst[_p][:, :], tmp_f[:, :], xres[_p][:, :]), inc=b.dve)
                    rel_dve[oc] = r1
                    b.wait("sync", b.dve, v2_done[k])
                    store_done[k] = b.dma("sync", b.so, dst_tile(oc, tt), outst[p][:, :])
                    kk = k + 2
                    if kk < n_oc * NT:
                        oc2, tt2 = divmod(kk, NT)
                        xres_ld[kk] = b.dma("sync", b.sx, xres[kk % 2][:, :], res_tile(oc2, tt2))
            elif kind in ("q", "k"):
                nb = nbq if kind == "q" else nbk
                b1 = bq1 if kind == "q" else bk1
                stage = qkvst[oc % 2]
                for tt in range(NT):
                    k = oc * NT + tt
                    tp = k % 2
                    b.wait("vector", b.pe, grp[oc])
                    if k >= 2:
                        b.wait("vector", b.act, prev_a1[tp])
                    z0 = b.op("vector", lambda e, _t=tt, _b=base, _tp=tp: e.tensor_tensor(
                        xres[_tp][:, :], psA[_b + _t][:, :],
                        a_bc[:, _t * TW:(_t + 1) * TW], ALU.mult), inc=b.dve)
                    rel_dve[oc] = z0
                    if tt == 0 and oc >= 1:
                        b.wait("vector", b.so, store_done[oc - 1])
                    b.wait("vector", b.dve, z0)
                    r1 = b.op("vector", lambda e, _oc=oc, _b1=b1, _tp=tp:
                              e.tensor_scalar(outst[_tp][:, :], xres[_tp][:, :],
                                              _b1[:, _oc:_oc + 1], None, ALU.add), inc=b.dve)
                    b.wait("scalar", b.dve, z0)
                    if k >= 1:
                        b.wait("scalar", b.act, prev_a2)
                    a1 = b.op("scalar", lambda e, _oc=oc, _nb=nb, _tp=tp: e.activation(
                        tmp_f[:, :], xres[_tp][:, :], AF.Relu,
                        bias=_nb[:, _oc:_oc + 1], scale=-1.0), inc=b.act)
                    b.wait("scalar", b.act, a1)
                    if k >= 1:
                        b.wait("scalar", b.dve, v2_done[k - 1])
                    a2 = b.op("scalar", lambda e: e.activation(
                        tmp_g[:, :], tmp_f[:, :], AF.Exp, bias=0.0, scale=-1.0), inc=b.act)
                    prev_a1[tp] = a1
                    prev_a2 = a2
                    b.wait("vector", b.act, a2)
                    b.wait("vector", b.dve, r1)
                    v2_done[k] = b.op("vector", lambda e, _tp=tp, _st=stage, _t=tt: e.tensor_tensor(
                        _st[:, _t * TW:(_t + 1) * TW], tmp_g[:, :], outst[_tp][:, :], ALU.max),
                        inc=b.dve)
                if kind == "k":
                    b.wait("vector", b.dve, v2_done[oc * NT + NT - 1])
                    b.op("vector", lambda e, _oc=oc, _st=stage: e.reduce_sum(
                        ksum_sb[:, _oc:_oc + 1], _st[:, :], axis=AXX))
                b.wait("sync", b.dve, v2_done[oc * NT + NT - 1])
                store_done[oc] = b.dma("sync", b.so, store_chunk(oc), stage[:, :])
            elif kind == "v":
                vb = bv_sb if n_oc == KC else bq_sb
                stage = qkvst[oc % 2]
                for tt in range(NT):
                    k = oc * NT + tt
                    tp = k % 2
                    b.wait("vector", b.pe, grp[oc])
                    if tt == 0 and oc >= 1:
                        b.wait("vector", b.so, store_done[oc - 1])
                    if k >= 2:
                        b.wait("vector", b.dve, v2_done[k - 2])
                    z0 = b.op("vector", lambda e, _t=tt, _b=base, _tp=tp: e.tensor_tensor(
                        xres[_tp][:, :], psA[_b + _t][:, :],
                        a_bc[:, _t * TW:(_t + 1) * TW], ALU.mult), inc=b.dve)
                    b.wait("vector", b.dve, z0)
                    r1 = b.op("vector", lambda e, _oc=oc, _t=tt, _st=stage, _vb=vb, _tp=tp:
                              e.tensor_scalar(_st[:, _t * TW:(_t + 1) * TW], xres[_tp][:, :],
                                              _vb[:, _oc:_oc + 1], None, ALU.add), inc=b.dve)
                    rel_dve[oc] = z0
                    v2_done[k] = r1
                b.wait("sync", b.dve, v2_done[oc * NT + NT - 1])
                store_done[oc] = b.dma("sync", b.so, store_chunk(oc), stage[:, :])
        return store_done

    x_full = lambda dc: xT[dc * 128:(dc + 1) * 128, :]
    x1_full = lambda dc: x1_s[dc, :, :]
    x2_full = lambda dc: x2_s[dc, :, :]
    out_tile = lambda oc, tt: out[oc * 128:(oc + 1) * 128, tt * TW:(tt + 1) * TW]

    ln_phase(x_full)
    u_ready = lora_u_phase()
    if STAGE == 1:
        d1 = out_tile
    else:
        d1 = lambda oc, tt: x1_s[oc, :, tt * TW:(tt + 1) * TW]
    mm_sweep("ffn", DC, lambda oc: wf_t[oc], dst_tile=d1,
             res_tile=lambda oc, tt: xT[oc * 128:(oc + 1) * 128, tt * TW:(tt + 1) * TW],
             bias=bf_sb, scale=0.5, lora=True)

    if STAGE >= 2:
        x1_store_total = b.so.n
        b.wait("gpsimd", b.so, x1_store_total)
        ln_phase(x1_full)
        if SUB != "ln2":
            mm_sweep("k", KC, lambda oc: wk_t[oc], store_chunk=lambda oc: k_s[oc])
            mm_sweep("v", KC, lambda oc: wv_t[oc], store_chunk=lambda oc: v_s[oc])
        kv_store_total = b.so.n

        if SUB in ("ln2", "qkv"):
            b.wait("sync", b.so, b.so.n)
            b.wait("sync", b.sx, b.sx.n)
            block = es.enter_context(nc.Block())
            with es:
                _emit_all(block, b)
            return nc
        b.wait("gpsimd", b.so, kv_store_total)
        kv_mm = None
        tr_done = {}
        for g in range(KC):
            if g >= 1:
                b.wait("gpsimd", b.pe, tr_done[g - 1])
            b.dma("gpsimd", b.gq, qkvst[0][:, :], k_s[g])
            kvld = b.dma("gpsimd", b.gq, qkvst[1][:, :], v_s[g])
            b.wait("tensor", b.gq, kvld)
            for lt in range(16):
                tb = (lt % 2) * 2
                b.wait("tensor", b.dve, b.dve.n)
                b.op("tensor", lambda e, _lt=lt, _tb=tb: e.transpose(
                    psA[_tb][:, 0:64].bitcast(BF16), qkvst[0][:, _lt * 128:(_lt + 1) * 128],
                    ident_sb[:, :]))
                t2 = b.op("tensor", lambda e, _lt=lt, _tb=tb: e.transpose(
                    psA[_tb + 1][:, 0:64].bitcast(BF16), qkvst[1][:, _lt * 128:(_lt + 1) * 128],
                    ident_sb[:, :]),
                    inc=b.pe)
                b.wait("vector", b.pe, t2)
                b.op("vector", lambda e, _tb=tb, _p=lt % 2: e.tensor_copy(
                    kT_t[_p][:, :], psA[_tb][:, 0:64].bitcast(BF16)))
                cv = b.op("vector", lambda e, _tb=tb, _p=lt % 2: e.tensor_copy(
                    vT_t[_p][:, :], psA[_tb + 1][:, 0:64].bitcast(BF16)), inc=b.dve)
                b.wait("tensor", b.dve, cv)
                kv_mm = b.op("tensor", lambda e, _g=g, _lt=lt, _p=lt % 2: e.matmul(
                    psA[4 + (_g // 4)][:, (_g % 4) * 128:(_g % 4 + 1) * 128],
                    kT_t[_p][:, :], vT_t[_p][:, :], start=(_lt == 0), stop=(_lt == 15)),
                    inc=(b.pe if lt == 15 else None))
            tr_done[g] = kv_mm
        b.wait("vector", b.pe, kv_mm)
        b.op("vector", lambda e: e.tensor_copy(kvt_sb[:, 0:512], psA[4][:, :]))
        kvt_ev = b.op("vector", lambda e: e.tensor_copy(kvt_sb[:, 512:1024], psA[5][:, :]),
                      inc=b.dve)
        if SUB == "kv":
            b.wait("sync", b.dve, kvt_ev)
            b.wait("sync", b.so, b.so.n)
            b.wait("sync", b.sx, b.sx.n)
            block = es.enter_context(nc.Block())
            with es:
                _emit_all(block, b)
            return nc
        b.wait("sync", b.dve, kvt_ev)
        b.dma("sync", b.so, cc_in[:, 0:1024], kvt_sb[:, :])
        pack_done = b.dma("sync", b.so, cc_in[:, 1024:1032], ksum_sb[:, :])
        b.wait("gpsimd", b.so, pack_done)
        ccv = b.op("gpsimd", lambda e: e.collective_compute(
            "AllReduce", ALU.add, replica_groups=[list(range(NCORES))],
            ins=[cc_in.opt()], outs=[cc_out.opt()]), inc=b.cc)
        mm_sweep("q", NH, lambda oc: wq_t[oc], store_chunk=lambda oc: q_s[oc])
        b.wait("gpsimd", b.cc, ccv)
        b.dma("gpsimd", b.gq, kvt_sb[:, :], cc_out[:, 0:1024])
        cc_ld = b.dma("gpsimd", b.gq, rn_a[:, :], cc_out[:, 1024:1032])
        b.wait("vector", b.gq, cc_ld)
        rn0 = b.op("vector", lambda e: e.tensor_scalar(rn_b[:, :], rn_a[:, :], EPS_ATT, None, ALU.add),
                   inc=b.dve)
        b.wait("vector", b.dve, rn0)
        rn_done = b.op("vector", lambda e: e.reciprocal(rn_a[:, :], rn_b[:, :]), inc=b.dve)
        b.wait("sync", b.dve, rn_done)
        rn_ld = None
        for g in range(KC):
            rn_ld = b.dma("sync", b.sx, rows[0:1, g * 128:(g + 1) * 128], rn_a[:, g:g + 1])
        b.wait("tensor", b.sx, rn_ld)
        b.wait("tensor", b.dve, b.dve.n)
        b.op("tensor", lambda e: e.matmul(psA[0][:, :], ones_r[:, :], rows[0:1, 0:512],
                                          start=True, stop=True))
        rbc = b.op("tensor", lambda e: e.matmul(psA[1][:, :], ones_r[:, :], rows[0:1, 512:1024],
                                                start=True, stop=True), inc=b.pe)
        b.wait("vector", b.pe, rbc)
        b.op("vector", lambda e: e.tensor_mul(kvtn[:, 0:512], kvt_sb[:, 0:512], psA[0][:, :]))
        kvtn_done = b.op("vector", lambda e: e.tensor_mul(kvtn[:, 512:1024],
                                                          kvt_sb[:, 512:1024], psA[1][:, :]),
                         inc=b.dve)

        if SUB == "cc":
            b.wait("sync", b.dve, kvtn_done)
            b.wait("sync", b.so, b.so.n)
            b.wait("sync", b.sx, b.sx.n)
            block = es.enter_context(nc.Block())
            with es:
                _emit_all(block, b)
            return nc
        b.wait("sync", b.so, b.so.n)
        b.wait("sync", b.pe, b.pe.n)
        qld = {}
        for h in range(2):
            qld[h] = b.dma("sync", b.sx, qkvst[h % 2][:, :], q_s[h])
        head_pe = {}
        head_act = {}
        for h in range(NH):
            g = h // 4
            base = 4 * (h % 2)
            b.wait("tensor", b.sx, qld[min(h + 1, NH - 1)])
            if h == 0:
                b.wait("tensor", b.dve, kvtn_done)
            if h >= 2:
                b.wait("tensor", b.act, head_act[h - 2])
            mm = None
            for tt in range(NT):
                mm = b.op("tensor", lambda e, _h=h, _g=g, _t=tt, _b=base: e.matmul(
                    psA[_b + _t][:, :], kvtn[:, _g * 128:(_g + 1) * 128],
                    qkvst[_h % 2][:, _t * TW:(_t + 1) * TW], start=True, stop=True),
                    inc=(b.pe if tt == NT - 1 else None))
            head_pe[h] = mm
            if h + 2 < NH:
                b.wait("sync", b.pe, head_pe[h])
                qld[h + 2] = b.dma("sync", b.sx, qkvst[h % 2][:, :], q_s[h + 2])
            b.wait("scalar", b.pe, head_pe[h])
            aa = None
            for tt in range(NT):
                aa = b.op("scalar", lambda e, _h=h, _t=tt, _b=base: e.copy(
                    h_buf[:, _h * S + _t * TW:_h * S + (_t + 1) * TW], psA[_b + _t][:, :]),
                    inc=(b.act if tt == NT - 1 else None))
            head_act[h] = aa

        if SUB == "attn":
            b.wait("sync", b.act, b.act.n)
            b.wait("sync", b.so, b.so.n)
            b.wait("sync", b.sx, b.sx.n)
            block = es.enter_context(nc.Block())
            with es:
                _emit_all(block, b)
            return nc
        d2 = out_tile if STAGE == 2 else (lambda oc, tt: x2_s[oc, :, tt * TW:(tt + 1) * TW])
        mm_sweep("ffn", DC, lambda oc: wo_t[oc], dst_tile=d2,
                 res_tile=lambda oc, tt: x1_s[oc, :, tt * TW:(tt + 1) * TW],
                 bias=bo_sb, scale=None, lora=False, res_ready=x1_store_total)

    if STAGE >= 3:
        x2_store_total = b.so.n
        b.wait("gpsimd", b.so, x2_store_total)
        ln_phase(x2_full)
        lora_u_phase()
        mm_sweep("ffn", DC, lambda oc: wf_t[oc], dst_tile=out_tile,
                 res_tile=lambda oc, tt: x2_s[oc, :, tt * TW:(tt + 1) * TW],
                 bias=bf_sb, scale=0.5, lora=True, res_ready=x2_store_total)

    b.wait("sync", b.so, b.so.n)
    b.wait("sync", b.sx, b.sx.n)

    block = es.enter_context(nc.Block())
    with es:
        _emit_all(block, b)
    return nc


def kernel(**inputs):
    x = np.asarray(inputs["L0_attn_hidden_states"], dtype=np.float32)[0]
    Wq = np.asarray(inputs["L0_attn_Wq"], dtype=np.float32)
    bq = np.asarray(inputs["L0_attn_bq"], dtype=np.float32)
    Wk = np.asarray(inputs["L0_attn_Wk"], dtype=np.float32)
    bk = np.asarray(inputs["L0_attn_bk"], dtype=np.float32)
    Wv = np.asarray(inputs["L0_attn_Wv"], dtype=np.float32)
    bv = np.asarray(inputs["L0_attn_bv"], dtype=np.float32)
    Wo = np.asarray(inputs["L0_attn_Wo"], dtype=np.float32)
    bo = np.asarray(inputs["L0_attn_bo"], dtype=np.float32)
    Wf = np.asarray(inputs["L0_ffn_weight"], dtype=np.float32)
    bf_ = np.asarray(inputs["L0_ffn_bias"], dtype=np.float32)
    lA = np.asarray(inputs["L0_ffn_lora_A"], dtype=np.float32)
    lB = np.asarray(inputs["L0_ffn_lora_B"], dtype=np.float32)
    lsi = np.asarray(inputs["L0_ffn_lora_start_indices"]).astype(np.int32).reshape(1, 4)

    def tile4(W, noc):
        return np.ascontiguousarray(W.reshape(noc, 128, DC, 128).transpose(0, 3, 2, 1))

    def brow(v_, nch):
        return np.ascontiguousarray(v_.reshape(nch, 128).T)

    common = {
        "wq_t": tile4(Wq, DC), "wo_t": tile4(Wo, DC), "wf_t": tile4(Wf, DC),
        "wk_t": tile4(Wk, KC), "wv_t": tile4(Wv, KC),
        "bq_h": brow(bq, DC), "bo_h": brow(bo, DC), "bf_h": brow(bf_, DC),
        "bk_h": brow(bk, KC), "bv_h": brow(bv, KC),
        "la_h": np.ascontiguousarray(lA.reshape(3, RK, DC, 128).transpose(3, 0, 2, 1)),
        "lb_h": np.ascontiguousarray(lB.reshape(3, DC, 128, RK).transpose(3, 0, 1, 2)),
        "lsi_h": lsi,
        "iota_h": np.arange(3, dtype=np.float32).reshape(1, 3),
        "ident_h": np.eye(128, dtype=np.float32),
    }
    xTfull = np.ascontiguousarray(x.T)
    in_maps = []
    for c in range(NCORES):
        m = dict(common)
        m["xT"] = np.ascontiguousarray(xTfull[:, c * S:(c + 1) * S])
        in_maps.append(m)

    nc = build_nc()
    res = run_bass_kernel_spmd(nc, in_maps, core_ids=list(range(NCORES)))
    full = np.concatenate([res.results[c]["out"] for c in range(NCORES)], axis=1)
    return np.ascontiguousarray(full.T).reshape(1, 16384, D).astype(np.float32)


# version 07
# speedup vs baseline: 1.0089x; 1.0089x over previous
import os
import contextlib
import numpy as np

import concourse.bass as bass
import concourse.mybir as mybir
from concourse.bass_utils import run_bass_kernel_spmd

F32 = mybir.dt.float32
BF16 = mybir.dt.bfloat16
I32 = mybir.dt.int32
AF = mybir.ActivationFunctionType
ALU = mybir.AluOpType
AXX = mybir.AxisListType.X

NCORES = 8
D = 4096
DC = 32
S = 2048
NT = 4
TW = 512
KD = 1024
KC = 8
NH = 32
RK = 16
EPS_LN = 1e-5
EPS_ATT = 1e-10

STAGE = int(os.environ.get("ATHENA_STAGE", "3"))
SUB = os.environ.get("ATHENA_SUB", "full")

ENGINES = ("tensor", "vector", "scalar", "sync", "gpsimd")


class Sem:

    def __init__(self, es, nc, name):
        self.h = es.enter_context(nc.semaphore(name))
        self.name = name
        self.n = 0
        self.bounds = {0}

    def inc(self, by):
        assert not getattr(self, "need_mark", False), \
            (self.name, "dma issued after an unboundaried wait resolved to total", self.n)
        self.n += by
        return self.n

    def mark(self):
        self.bounds.add(self.n)
        self.need_mark = False

    def wait_val(self, v):
        cands = [x for x in self.bounds if x >= v]
        if not cands and self.n >= v:
            self.need_mark = True
            return self.n
        assert cands, ("no boundary for wait", self.name, v, self.n)
        return min(cands)


class B:
    def __init__(self, nc, es):
        self.nc = nc
        self.ops = {e: [] for e in ENGINES}
        self.pe = Sem(es, nc, "pe")
        self.dve = Sem(es, nc, "dve")
        self.act = Sem(es, nc, "act")
        self.cc = Sem(es, nc, "cc")
        self.w0 = Sem(es, nc, "w0")
        self.w1 = Sem(es, nc, "w1")
        self.gq = Sem(es, nc, "gq")
        self.sx = Sem(es, nc, "sx")
        self.so = Sem(es, nc, "so")
        self.lq = [Sem(es, nc, f"lq{i}") for i in range(8)]
        self._dma = {"gpsimd": [self.w0, self.w1, self.gq] + self.lq,
                     "sync": [self.sx, self.so]}
        self._is_dma = {self.w0, self.w1, self.gq, self.sx, self.so, *self.lq}

    def wait(self, eng, sem, v):
        val = sem.wait_val(v) if sem in self._is_dma else v
        assert val <= sem.n, (eng, sem.name, v, sem.n)
        self.ops[eng].append(lambda e, _s=sem.h, _v=val: e.wait_ge(_s, _v))
        for s in self._dma.get(eng, ()):
            s.mark()

    def dma(self, eng, sem, out_ap, in_ap):
        self.ops[eng].append(
            lambda e, _o=out_ap, _i=in_ap, _s=sem.h: e.dma_start(out=_o, in_=_i).then_inc(_s, 16))
        return sem.inc(16)

    def op(self, eng, fn, inc=None):
        if inc is None:
            self.ops[eng].append(fn)
            return None
        self.ops[eng].append(lambda e, _f=fn, _s=inc.h: _f(e).then_inc(_s, 1))
        v = inc.inc(1)
        inc.mark()
        return v


def _emit_all(block, b):
    @block.tensor
    def _(e):
        for f in b.ops["tensor"]:
            f(e)

    @block.vector
    def _(e):
        for f in b.ops["vector"]:
            f(e)

    @block.scalar
    def _(e):
        for f in b.ops["scalar"]:
            f(e)

    @block.sync
    def _(e):
        for f in b.ops["sync"]:
            f(e)

    @block.gpsimd
    def _(e):
        for f in b.ops["gpsimd"]:
            f(e)


def build_nc():
    nc = bass.Bass("TRN2", target_bir_lowering=False)
    es = contextlib.ExitStack()
    ap = lambda t: t.ap()

    xT = ap(nc.dram_tensor("xT", [D, S], F32, kind="ExternalInput"))
    wq_t = ap(nc.dram_tensor("wq_t", [DC, 128, DC, 128], F32, kind="ExternalInput"))
    wo_t = ap(nc.dram_tensor("wo_t", [DC, 128, DC, 128], F32, kind="ExternalInput"))
    wf_t = ap(nc.dram_tensor("wf_t", [DC, 128, DC, 128], F32, kind="ExternalInput"))
    wk_t = ap(nc.dram_tensor("wk_t", [KC, 128, DC, 128], F32, kind="ExternalInput"))
    wv_t = ap(nc.dram_tensor("wv_t", [KC, 128, DC, 128], F32, kind="ExternalInput"))
    bq_h = ap(nc.dram_tensor("bq_h", [128, DC], F32, kind="ExternalInput"))
    bo_h = ap(nc.dram_tensor("bo_h", [128, DC], F32, kind="ExternalInput"))
    bf_h = ap(nc.dram_tensor("bf_h", [128, DC], F32, kind="ExternalInput"))
    bk_h = ap(nc.dram_tensor("bk_h", [128, KC], F32, kind="ExternalInput"))
    bv_h = ap(nc.dram_tensor("bv_h", [128, KC], F32, kind="ExternalInput"))
    la_h = ap(nc.dram_tensor("la_h", [128, 3, DC, RK], F32, kind="ExternalInput"))
    lb_h = ap(nc.dram_tensor("lb_h", [RK, 3, DC, 128], F32, kind="ExternalInput"))
    lsi_h = ap(nc.dram_tensor("lsi_h", [1, 4], I32, kind="ExternalInput"))
    iota_h = ap(nc.dram_tensor("iota_h", [1, 3], F32, kind="ExternalInput"))
    ident_h = ap(nc.dram_tensor("ident_h", [128, 128], F32, kind="ExternalInput"))

    out = ap(nc.dram_tensor("out", [D, S], F32, kind="ExternalOutput"))

    x1_s = ap(nc.dram_tensor("x1_s", [DC, 128, S], F32))
    x2_s = ap(nc.dram_tensor("x2_s", [DC, 128, S], F32))
    q_s = ap(nc.dram_tensor("q_s", [128, NH * S], BF16))
    k_s = ap(nc.dram_tensor("k_s", [KC, 128, S], BF16))
    v_s = ap(nc.dram_tensor("v_s", [KC, 128, S], BF16))
    cc_in = ap(nc.dram_tensor("cc_in", [128, 1032], F32))
    cc_out = ap(nc.dram_tensor("cc_out", [128, 1032], F32))

    sb = lambda n, shp, dt: es.enter_context(nc.sbuf_tensor(n, shp, dt))
    h_buf = sb("h_buf", [128, DC * S], BF16)
    wslab = [sb(f"wslab{i}", [128, DC * 128], BF16) for i in range(2)]
    a_bc = sb("a_bc", [128, S], BF16)
    nm_bc = sb("nm_bc", [128, S], BF16)
    rows = sb("rows", [1, 2048], F32)
    qkvst = [sb(f"qkvst{i}", [128, S], BF16) for i in range(2)]
    u_sb = sb("u_sb", [RK, S], BF16)
    a_eff = sb("a_eff", [128, DC * RK], BF16)
    b_eff = sb("b_eff", [RK, D], BF16)
    bq_sb = sb("bq_sb", [128, DC], F32)
    bo_sb = sb("bo_sb", [128, DC], F32)
    bf_sb = sb("bf_sb", [128, DC], F32)
    bfh_sb = sb("bfh_sb", [128, DC], F32)
    bk_sb = sb("bk_sb", [128, KC], F32)
    bv_sb = sb("bv_sb", [128, KC], F32)
    nbq = sb("nbq", [128, DC], F32)
    bq1 = sb("bq1", [128, DC], F32)
    nbk = sb("nbk", [128, KC], F32)
    bk1 = sb("bk1", [128, KC], F32)
    kvt_sb = sb("kvt_sb", [128, KD], F32)
    kvtn = sb("kvtn", [128, KD], BF16)
    ksum_sb = sb("ksum_sb", [128, KC], F32)
    rn_a = sb("rn_a", [128, KC], F32)
    rn_b = sb("rn_b", [128, KC], F32)
    kT_t = [sb(f"kT{i}", [128, 128], BF16) for i in range(2)]
    vT_t = [sb(f"vT{i}", [128, 128], BF16) for i in range(2)]
    ident_sb = sb("ident_sb", [128, 128], BF16)
    ones_c = sb("ones_c", [128, 1], BF16)
    ones_r = sb("ones_r", [1, 128], F32)
    iota_sb = sb("iota_sb", [1, 3], F32)
    lsi_sb = sb("lsi_sb", [1, 4], I32)
    lsw = sb("lsw", [1, 20], F32)
    m_row = sb("m_row", [1, 3], F32)
    m_sb = sb("m_sb", [128, 3], F32)

    es_setup = contextlib.ExitStack()
    la_st = es_setup.enter_context(nc.sbuf_tensor("la_st", [128, 3 * DC * RK], BF16))
    lb_st = es_setup.enter_context(nc.sbuf_tensor("lb_st", [RK, S], BF16))
    tmp_a = es_setup.enter_context(nc.sbuf_tensor("tmp_a", [128, DC * RK], BF16))
    tmp_b = es_setup.enter_context(nc.sbuf_tensor("tmp_b", [RK, S], BF16))

    psA = [es.enter_context(nc.psum_tensor(f"ps{i}", [128, 512], F32)) for i in range(8)]

    b = B(nc, es)

    b.dma("sync", b.sx, bq_sb[:, :], bq_h)
    b.dma("sync", b.sx, bo_sb[:, :], bo_h)
    b.dma("sync", b.sx, bf_sb[:, :], bf_h)
    b.dma("sync", b.sx, bk_sb[:, :], bk_h)
    b.dma("sync", b.sx, bv_sb[:, :], bv_h)
    b.dma("sync", b.sx, lsi_sb[:, :], lsi_h)
    setup_sx = b.dma("sync", b.sx, iota_sb[:, :], iota_h)
    b.dma("gpsimd", b.gq, ident_sb[:, :], ident_h)
    setup_la = b.dma("gpsimd", b.gq,
                     la_st[:, :].rearrange("p (i c r) -> p i c r", i=3, c=DC), la_h)

    b.op("vector", lambda e: e.memset(ones_c[:, :], 1.0))
    b.op("vector", lambda e: e.memset(ones_r[:, :], 1.0))
    b.wait("vector", b.sx, setup_sx)
    b.op("vector", lambda e: e.tensor_scalar(nbq[:, :], bq_sb[:, :], -1.0, None, ALU.mult))
    b.op("vector", lambda e: e.tensor_scalar(bq1[:, :], bq_sb[:, :], 1.0, None, ALU.add))
    b.op("vector", lambda e: e.tensor_scalar(bfh_sb[:, :], bf_sb[:, :], 0.5, None, ALU.mult))
    b.op("vector", lambda e: e.tensor_scalar(nbk[:, :], bk_sb[:, :], -1.0, None, ALU.mult))
    b.op("vector", lambda e: e.tensor_scalar(bk1[:, :], bk_sb[:, :], 1.0, None, ALU.add))
    L = lsw
    _vc = [None]

    def VC(fn):
        if _vc[0] is not None:
            b.wait("vector", b.dve, _vc[0])
        _vc[0] = b.op("vector", fn, inc=b.dve)
        return _vc[0]

    VC(lambda e: e.tensor_scalar(L[0:1, 0:4], lsi_sb[:, :], 0.0, None, ALU.is_le))
    VC(lambda e: e.tensor_scalar(L[0:1, 4:5], L[0:1, 3:4], 3.0, None, ALU.add))
    VC(lambda e: e.tensor_scalar(L[0:1, 5:6], L[0:1, 1:2], 2.0, None, ALU.mult))
    VC(lambda e: e.tensor_scalar(L[0:1, 6:7], L[0:1, 1:2], -1.0, 1.0, ALU.mult, ALU.add))
    VC(lambda e: e.tensor_mul(L[0:1, 7:8], L[0:1, 6:7], L[0:1, 0:1]))
    VC(lambda e: e.tensor_add(L[0:1, 8:9], L[0:1, 5:6], L[0:1, 7:8]))
    VC(lambda e: e.tensor_mul(L[0:1, 9:10], L[0:1, 2:3], L[0:1, 4:5]))
    VC(lambda e: e.tensor_scalar(L[0:1, 10:11], L[0:1, 2:3], -1.0, 1.0, ALU.mult, ALU.add))
    VC(lambda e: e.tensor_mul(L[0:1, 11:12], L[0:1, 10:11], L[0:1, 8:9]))
    VC(lambda e: e.tensor_add(L[0:1, 12:13], L[0:1, 9:10], L[0:1, 11:12]))
    VC(lambda e: e.tensor_scalar(L[0:1, 13:14], L[0:1, 12:13], -1.0, None, ALU.add))
    VC(lambda e: e.tensor_scalar(L[0:1, 14:15], L[0:1, 13:14], 1.0, None, ALU.is_ge))
    VC(lambda e: e.tensor_scalar(L[0:1, 15:16], L[0:1, 13:14], -1.0, 0.0, ALU.add, ALU.max))
    VC(lambda e: e.tensor_scalar(L[0:1, 16:17], L[0:1, 15:16], 2.0, None, ALU.min))
    VC(lambda e: e.tensor_scalar(L[0:1, 17:20], iota_sb[:, :], L[0:1, 16:17], None, ALU.is_equal))
    mask_done = VC(lambda e: e.tensor_scalar(m_row[:, :], L[0:1, 17:20], L[0:1, 14:15], None, ALU.mult))
    b.wait("tensor", b.dve, mask_done)
    mbc = b.op("tensor", lambda e: e.matmul(psA[0][:, 0:3], ones_r[:, :], m_row[:, :],
                                            start=True, stop=True), inc=b.pe)
    b.wait("vector", b.pe, mbc)
    msb_done = b.op("vector", lambda e: e.tensor_copy(m_sb[:, :], psA[0][:, 0:3]), inc=b.dve)
    b.wait("gpsimd", b.dve, msb_done)

    b.wait("vector", b.gq, setup_la)
    b.wait("vector", b.dve, msb_done)
    _a1 = b.op("vector", lambda e: e.tensor_scalar(a_eff[:, :], la_st[:, 0:DC * RK],
                                                   m_sb[:, 0:1], None, ALU.mult), inc=b.dve)
    b.wait("vector", b.dve, _a1)
    _a2 = b.op("vector", lambda e: e.scalar_tensor_tensor(tmp_a[:, :], la_st[:, DC * RK:2 * DC * RK],
                                                          m_sb[:, 1:2], a_eff[:, :], ALU.mult, ALU.add),
               inc=b.dve)
    b.wait("vector", b.dve, _a2)
    aeff_last = b.op("vector",
                     lambda e: e.scalar_tensor_tensor(a_eff[:, :], la_st[:, 2 * DC * RK:3 * DC * RK],
                                                      m_sb[:, 2:3], tmp_a[:, :], ALU.mult, ALU.add),
                     inc=b.dve)
    b.wait("gpsimd", b.dve, aeff_last)
    done = None
    for hh in range(2):
        cs = slice(hh * 2048, (hh + 1) * 2048)
        for i in range(3):
            ld = b.dma("gpsimd", b.gq,
                       lb_st[:, :].rearrange("r (c o) -> r c o", c=16),
                       lb_h[:, i, hh * 16:(hh + 1) * 16, :])
            b.wait("vector", b.gq, ld)
            if i > 0 or hh > 0:
                b.wait("vector", b.dve, done)
            if i == 0:
                done = b.op("vector", lambda e, _c=cs: e.tensor_scalar(
                    b_eff[:, _c], lb_st[:, :], m_sb[0:RK, 0:1], None, ALU.mult), inc=b.dve)
            elif i == 1:
                done = b.op("vector", lambda e, _c=cs: e.scalar_tensor_tensor(
                    tmp_b[:, :], lb_st[:, :], m_sb[0:RK, 1:2], b_eff[:, _c], ALU.mult, ALU.add),
                    inc=b.dve)
            else:
                done = b.op("vector", lambda e, _c=cs: e.scalar_tensor_tensor(
                    b_eff[:, _c], lb_st[:, :], m_sb[0:RK, 2:3], tmp_b[:, :], ALU.mult, ALU.add),
                    inc=b.dve)
            b.wait("gpsimd", b.dve, done)

    setup_gq_total = b.gq.n
    setup_dve_total = b.dve.n
    es_setup.close()
    xres = [es.enter_context(nc.sbuf_tensor(f"xres{i}", [128, TW], F32)) for i in range(2)]
    outst = [es.enter_context(nc.sbuf_tensor(f"outst{i}", [128, TW], F32)) for i in range(2)]
    xsq = [es.enter_context(nc.sbuf_tensor(f"xsq{i}", [128, 1024], BF16)) for i in range(2)]
    tmp_f = es.enter_context(nc.sbuf_tensor("tmp_f", [128, TW], F32))
    tmp_g = es.enter_context(nc.sbuf_tensor("tmp_g", [128, TW], F32))

    def h_slice(dc, half):
        return h_buf[:, dc * S + half * 1024: dc * S + (half + 1) * 1024]

    def ln_phase(src_full):
        pe_entry = b.pe.n
        b.wait("gpsimd", b.pe, pe_entry)
        b.wait("gpsimd", b.act, b.act.n)
        order = [(dc, half) for half in range(2) for dc in range(DC)]
        ld = {}
        for dcp in range(8):
            ld[dcp] = b.dma("gpsimd", b.lq[dcp % 8],
                            h_buf[:, dcp * S:(dcp + 1) * S], src_full(dcp))
        mm_done = {}
        cp_dve = {}
        bsum = (psA[4], psA[6])
        bsq = (psA[5], psA[7])
        for ki, (dc, half) in enumerate(order):
            p = ki % 2
            b.wait("scalar", b.lq[dc % 8], ld[dc])
            if ki >= 2:
                b.wait("scalar", b.pe, mm_done[order[ki - 2]])
            else:
                b.wait("scalar", b.pe, pe_entry)
            sq = b.op("scalar", lambda e, _p=p, _s=h_slice(dc, half): e.activation(
                xsq[_p][:, :], _s, AF.Square), inc=b.act)
            b.wait("tensor", b.lq[dc % 8], ld[dc])
            b.wait("tensor", b.act, sq)
            st, sp_ = (dc == 0), (dc == DC - 1)
            if ki == 0:
                b.wait("tensor", b.dve, b.dve.n)
            elif half == 1 and dc == 0:
                b.wait("tensor", b.dve, cp_dve[0])
            b.op("tensor", lambda e, _st=st, _sp=sp_, _o=bsum[0], _s=h_slice(dc, half): e.matmul(
                _o[0:1, :], ones_c[:, :], _s[:, 0:512], start=_st, stop=_sp))
            b.op("tensor", lambda e, _p=p, _st=st, _sp=sp_, _o=bsq[0]: e.matmul(
                _o[0:1, :], ones_c[:, :], xsq[_p][:, 0:512], start=_st, stop=_sp))
            b.op("tensor", lambda e, _st=st, _sp=sp_, _o=bsum[1], _s=h_slice(dc, half): e.matmul(
                _o[0:1, :], ones_c[:, :], _s[:, 512:1024], start=_st, stop=_sp))
            mm_done[(dc, half)] = b.op("tensor", lambda e, _p=p, _st=st, _sp=sp_, _o=bsq[1]: e.matmul(
                _o[0:1, :], ones_c[:, :], xsq[_p][:, 512:1024], start=_st, stop=_sp), inc=b.pe)
            if half == 0 and dc + 8 < DC:
                b.wait("gpsimd", b.pe, mm_done[(dc, half)])
                ld[dc + 8] = b.dma("gpsimd", b.lq[(dc + 8) % 8],
                                   h_buf[:, (dc + 8) * S:(dc + 9) * S], src_full(dc + 8))
            if dc != DC - 1:
                continue
            stats_pe = b.pe.n
            for j in range(2):
                S_ps, Q_ps = bsum[j], bsq[j]
                q0 = rows[0:1, 0:512]
                q1 = rows[0:1, 512:1024]
                q2 = rows[0:1, 1024:1536]
                q3 = rows[0:1, 1536:2048]
                b.wait("vector", b.pe, stats_pe if j == 0 else b.pe.n)
                c0 = b.op("vector", lambda e, _S=S_ps, _q=q0: e.tensor_scalar(
                    _q, _S[0:1, :], -1.0 / D, None, ALU.mult), inc=b.dve)
                c1 = b.op("vector", lambda e, _Q=Q_ps, _q=q1: e.tensor_scalar(
                    _q, _Q[0:1, :], 1.0 / D, EPS_LN, ALU.mult, ALU.add), inc=b.dve)
                b.wait("vector", b.dve, c1)
                v0 = b.op("vector", lambda e, _q0=q0, _q2=q2: e.tensor_mul(_q2, _q0, _q0),
                          inc=b.dve)
                b.wait("vector", b.dve, v0)
                v1 = b.op("vector", lambda e, _q1=q1, _q2=q2, _q3=q3: e.tensor_sub(_q3, _q1, _q2),
                          inc=b.dve)
                b.wait("scalar", b.dve, v1)
                a0 = b.op("scalar", lambda e, _q2=q2, _q3=q3: e.activation(_q2, _q3, AF.Ln),
                          inc=b.act)
                b.wait("scalar", b.act, a0)
                a1 = b.op("scalar", lambda e, _q1=q1, _q2=q2: e.activation(
                    _q1, _q2, AF.Exp, bias=0.0, scale=-0.5), inc=b.act)
                b.wait("tensor", b.dve, v1)
                b.wait("tensor", b.act, a1)
                b.op("tensor", lambda e, _S=S_ps, _q=q1: e.matmul(
                    _S[:, :], ones_r[:, :], _q, start=True, stop=True))
                bc = b.op("tensor", lambda e, _Q=Q_ps, _q=q0: e.matmul(
                    _Q[:, :], ones_r[:, :], _q, start=True, stop=True), inc=b.pe)
                b.wait("vector", b.pe, bc)
                cls = slice(half * 1024 + j * 512, half * 1024 + (j + 1) * 512)
                b.op("vector", lambda e, _S=S_ps, _c=cls: e.tensor_copy(a_bc[:, _c], _S[:, :]))
                b.op("vector", lambda e, _Q=Q_ps, _c=cls: e.tensor_copy(nm_bc[:, _c], _Q[:, :]),
                     inc=b.dve)
            cp_dve[half] = b.dve.n
            b.wait("vector", b.dve, cp_dve[half])
            nms = nm_bc[:, half * 1024:(half + 1) * 1024]
            for sdc in range(DC):
                b.op("vector", lambda e, _h=h_slice(sdc, half), _n=nms: e.tensor_tensor(
                    _h, _h, _n, ALU.add), inc=(b.dve if sdc == DC - 1 else None))

    def lora_u_phase():
        b.wait("tensor", b.dve, b.dve.n)
        b.wait("tensor", b.act, b.act.n)
        mm = None
        for dc in range(DC):
            for tt in range(NT):
                mm = b.op("tensor", lambda e, _dc=dc, _t=tt, _st=(dc == 0), _sp=(dc == DC - 1):
                          e.matmul(psA[4 + _t][0:RK, :], a_eff[:, _dc * RK:(_dc + 1) * RK],
                                   h_buf[:, _dc * S + _t * TW:_dc * S + (_t + 1) * TW],
                                   start=_st, stop=_sp),
                          inc=(b.pe if (dc == DC - 1 and tt == NT - 1) else None))
        b.wait("vector", b.pe, mm)
        uv = None
        for tt in range(NT):
            uv = b.op("vector", lambda e, _t=tt: e.tensor_copy(
                u_sb[:, _t * TW:(_t + 1) * TW], psA[4 + _t][0:RK, :]), inc=b.dve)
        return uv

    def mm_sweep(kind, n_oc, slab_src, dst_tile=None, res_tile=None, store_chunk=None,
                 bias=None, scale=None, lora=False, res_ready=0):
        slab_done = {}
        wsem = (b.w0, b.w1)
        b.wait("gpsimd", b.pe, b.pe.n)
        for oc in range(min(2, n_oc)):
            slab_done[oc] = b.dma("gpsimd", wsem[oc % 2],
                                  wslab[oc % 2][:, :].rearrange("p (c o) -> p c o", c=DC),
                                  slab_src(oc))
        grp = {}
        rel_dve = {}
        rel_act = {}
        prev_a1 = {}
        prev_a2 = None
        dve_entry = b.dve.n
        xres_ld = {}
        v2_done = {}
        store_done = {}
        if kind == "ffn":
            if res_ready:
                b.wait("sync", b.so, res_ready)
            b.wait("sync", b.pe, b.pe.n)
            b.wait("sync", b.gq, setup_gq_total)
            b.wait("sync", b.dve, setup_dve_total)
            for kk in range(2):
                xres_ld[kk] = b.dma("sync", b.sx, xres[kk % 2][:, :], res_tile(0, kk))
        for oc in range(n_oc):
            base = 4 * (oc % 2)
            b.wait("tensor", wsem[oc % 2], slab_done[oc])
            if oc == 0:
                b.wait("tensor", b.dve, b.dve.n)
                b.wait("tensor", b.act, b.act.n)
                b.wait("vector", b.dve, dve_entry)
            if oc >= 2:
                b.wait("tensor", b.dve, rel_dve[oc - 2])
                if (oc - 2) in rel_act:
                    b.wait("tensor", b.act, rel_act[oc - 2])
            for dc in range(DC):
                for tt in range(NT):
                    sp_ = (dc == DC - 1) and not lora
                    last = (not lora) and (dc == DC - 1) and (tt == NT - 1)
                    grp[oc] = b.op("tensor",
                                   lambda e, _oc=oc, _dc=dc, _t=tt, _st=(dc == 0), _sp=sp_, _b=base:
                                   e.matmul(psA[_b + _t][:, :],
                                            wslab[_oc % 2][:, _dc * 128:(_dc + 1) * 128],
                                            h_buf[:, _dc * S + _t * TW:_dc * S + (_t + 1) * TW],
                                            start=_st, stop=_sp),
                                   inc=(b.pe if last else None))
            if lora:
                for tt in range(NT):
                    grp[oc] = b.op("tensor", lambda e, _oc=oc, _t=tt, _b=base: e.matmul(
                        psA[_b + _t][:, :], b_eff[:, _oc * 128:(_oc + 1) * 128],
                        u_sb[:, _t * TW:(_t + 1) * TW], start=False, stop=True),
                        inc=(b.pe if tt == NT - 1 else None))
            if oc + 2 < n_oc:
                b.wait("gpsimd", b.pe, grp[oc])
                slab_done[oc + 2] = b.dma("gpsimd", wsem[oc % 2],
                                          wslab[oc % 2][:, :].rearrange("p (c o) -> p c o", c=DC),
                                          slab_src(oc + 2))
            if kind == "ffn":
                for tt in range(NT):
                    k = oc * NT + tt
                    p = k % 2
                    b.wait("vector", b.pe, grp[oc])
                    if k >= 1:
                        b.wait("vector", b.so, store_done[k - 1])
                    if scale is not None:
                        r1 = b.op("vector", lambda e, _oc=oc, _t=tt, _b=base:
                                  e.scalar_tensor_tensor(
                                      tmp_f[:, :], psA[_b + _t][:, :], scale,
                                      a_bc[:, _t * TW:(_t + 1) * TW], ALU.mult, ALU.mult),
                                  inc=b.dve)
                    else:
                        r1 = b.op("vector", lambda e, _oc=oc, _t=tt, _b=base: e.tensor_scalar(
                            tmp_f[:, :], psA[_b + _t][:, :], bias[:, _oc:_oc + 1], None,
                            ALU.add), inc=b.dve)
                    b.wait("vector", b.sx, xres_ld[min(k + 1, n_oc * NT - 1)])
                    b.wait("vector", b.dve, r1)
                    if scale is not None:
                        v2_done[k] = b.op("vector", lambda e, _p=p, _oc=oc:
                                          e.scalar_tensor_tensor(
                                              outst[_p][:, :], xres[_p][:, :],
                                              bfh_sb[:, _oc:_oc + 1], tmp_f[:, :],
                                              ALU.add, ALU.add), inc=b.dve)
                    else:
                        v2_done[k] = b.op("vector", lambda e, _p=p: e.tensor_add(
                            outst[_p][:, :], tmp_f[:, :], xres[_p][:, :]), inc=b.dve)
                    rel_dve[oc] = r1
                    b.wait("sync", b.dve, v2_done[k])
                    store_done[k] = b.dma("sync", b.so, dst_tile(oc, tt), outst[p][:, :])
                    kk = k + 2
                    if kk < n_oc * NT:
                        oc2, tt2 = divmod(kk, NT)
                        xres_ld[kk] = b.dma("sync", b.sx, xres[kk % 2][:, :], res_tile(oc2, tt2))
            elif kind in ("q", "k"):
                nb = nbq if kind == "q" else nbk
                b1 = bq1 if kind == "q" else bk1
                stage = qkvst[oc % 2]
                for tt in range(NT):
                    k = oc * NT + tt
                    tp = k % 2
                    b.wait("vector", b.pe, grp[oc])
                    if k >= 2:
                        b.wait("vector", b.act, prev_a1[tp])
                    z0 = b.op("vector", lambda e, _t=tt, _b=base, _tp=tp: e.tensor_tensor(
                        xres[_tp][:, :], psA[_b + _t][:, :],
                        a_bc[:, _t * TW:(_t + 1) * TW], ALU.mult), inc=b.dve)
                    rel_dve[oc] = z0
                    if tt == 0 and oc >= 1:
                        b.wait("vector", b.so, store_done[oc - 1])
                    b.wait("vector", b.dve, z0)
                    r1 = b.op("vector", lambda e, _oc=oc, _b1=b1, _tp=tp:
                              e.tensor_scalar(outst[_tp][:, :], xres[_tp][:, :],
                                              _b1[:, _oc:_oc + 1], None, ALU.add), inc=b.dve)
                    b.wait("scalar", b.dve, z0)
                    if k >= 1:
                        b.wait("scalar", b.act, prev_a2)
                    a1 = b.op("scalar", lambda e, _oc=oc, _nb=nb, _tp=tp: e.activation(
                        tmp_f[:, :], xres[_tp][:, :], AF.Relu,
                        bias=_nb[:, _oc:_oc + 1], scale=-1.0), inc=b.act)
                    b.wait("scalar", b.act, a1)
                    if k >= 1:
                        b.wait("scalar", b.dve, v2_done[k - 1])
                    a2 = b.op("scalar", lambda e: e.activation(
                        tmp_g[:, :], tmp_f[:, :], AF.Exp, bias=0.0, scale=-1.0), inc=b.act)
                    prev_a1[tp] = a1
                    prev_a2 = a2
                    b.wait("vector", b.act, a2)
                    b.wait("vector", b.dve, r1)
                    v2_done[k] = b.op("vector", lambda e, _tp=tp, _st=stage, _t=tt: e.tensor_tensor(
                        _st[:, _t * TW:(_t + 1) * TW], tmp_g[:, :], outst[_tp][:, :], ALU.max),
                        inc=b.dve)
                if kind == "k":
                    b.wait("vector", b.dve, v2_done[oc * NT + NT - 1])
                    b.op("vector", lambda e, _oc=oc, _st=stage: e.reduce_sum(
                        ksum_sb[:, _oc:_oc + 1], _st[:, :], axis=AXX))
                b.wait("sync", b.dve, v2_done[oc * NT + NT - 1])
                store_done[oc] = b.dma("sync", b.so, store_chunk(oc), stage[:, :])
            elif kind == "v":
                vb = bv_sb if n_oc == KC else bq_sb
                stage = qkvst[oc % 2]
                for tt in range(NT):
                    k = oc * NT + tt
                    tp = k % 2
                    b.wait("vector", b.pe, grp[oc])
                    if tt == 0 and oc >= 1:
                        b.wait("vector", b.so, store_done[oc - 1])
                    if k >= 2:
                        b.wait("vector", b.dve, v2_done[k - 2])
                    z0 = b.op("vector", lambda e, _t=tt, _b=base, _tp=tp: e.tensor_tensor(
                        xres[_tp][:, :], psA[_b + _t][:, :],
                        a_bc[:, _t * TW:(_t + 1) * TW], ALU.mult), inc=b.dve)
                    b.wait("vector", b.dve, z0)
                    r1 = b.op("vector", lambda e, _oc=oc, _t=tt, _st=stage, _vb=vb, _tp=tp:
                              e.tensor_scalar(_st[:, _t * TW:(_t + 1) * TW], xres[_tp][:, :],
                                              _vb[:, _oc:_oc + 1], None, ALU.add), inc=b.dve)
                    rel_dve[oc] = z0
                    v2_done[k] = r1
                b.wait("sync", b.dve, v2_done[oc * NT + NT - 1])
                store_done[oc] = b.dma("sync", b.so, store_chunk(oc), stage[:, :])
        return store_done

    x_full = lambda dc: xT[dc * 128:(dc + 1) * 128, :]
    x1_full = lambda dc: x1_s[dc, :, :]
    x2_full = lambda dc: x2_s[dc, :, :]
    out_tile = lambda oc, tt: out[oc * 128:(oc + 1) * 128, tt * TW:(tt + 1) * TW]

    ln_phase(x_full)
    u_ready = lora_u_phase()
    if STAGE == 1:
        d1 = out_tile
    else:
        d1 = lambda oc, tt: x1_s[oc, :, tt * TW:(tt + 1) * TW]
    mm_sweep("ffn", DC, lambda oc: wf_t[oc], dst_tile=d1,
             res_tile=lambda oc, tt: xT[oc * 128:(oc + 1) * 128, tt * TW:(tt + 1) * TW],
             bias=bf_sb, scale=0.5, lora=True)

    if STAGE >= 2:
        x1_store_total = b.so.n
        b.wait("gpsimd", b.so, x1_store_total)
        ln_phase(x1_full)
        if SUB != "ln2":
            mm_sweep("k", KC, lambda oc: wk_t[oc], store_chunk=lambda oc: k_s[oc])
            mm_sweep("v", KC, lambda oc: wv_t[oc], store_chunk=lambda oc: v_s[oc])
        kv_store_total = b.so.n

        if SUB in ("ln2", "qkv"):
            b.wait("sync", b.so, b.so.n)
            b.wait("sync", b.sx, b.sx.n)
            block = es.enter_context(nc.Block())
            with es:
                _emit_all(block, b)
            return nc
        b.wait("gpsimd", b.so, kv_store_total)
        kv_mm = None
        tr_done = {}
        for g in range(KC):
            if g >= 1:
                b.wait("gpsimd", b.pe, tr_done[g - 1])
            b.dma("gpsimd", b.gq, qkvst[0][:, :], k_s[g])
            kvld = b.dma("gpsimd", b.gq, qkvst[1][:, :], v_s[g])
            b.wait("tensor", b.gq, kvld)
            for lt in range(16):
                tb = (lt % 2) * 2
                b.wait("tensor", b.dve, b.dve.n)
                b.op("tensor", lambda e, _lt=lt, _tb=tb: e.transpose(
                    psA[_tb][:, 0:64].bitcast(BF16), qkvst[0][:, _lt * 128:(_lt + 1) * 128],
                    ident_sb[:, :]))
                t2 = b.op("tensor", lambda e, _lt=lt, _tb=tb: e.transpose(
                    psA[_tb + 1][:, 0:64].bitcast(BF16), qkvst[1][:, _lt * 128:(_lt + 1) * 128],
                    ident_sb[:, :]),
                    inc=b.pe)
                b.wait("vector", b.pe, t2)
                b.op("vector", lambda e, _tb=tb, _p=lt % 2: e.tensor_copy(
                    kT_t[_p][:, :], psA[_tb][:, 0:64].bitcast(BF16)))
                cv = b.op("vector", lambda e, _tb=tb, _p=lt % 2: e.tensor_copy(
                    vT_t[_p][:, :], psA[_tb + 1][:, 0:64].bitcast(BF16)), inc=b.dve)
                b.wait("tensor", b.dve, cv)
                kv_mm = b.op("tensor", lambda e, _g=g, _lt=lt, _p=lt % 2: e.matmul(
                    psA[4 + (_g // 4)][:, (_g % 4) * 128:(_g % 4 + 1) * 128],
                    kT_t[_p][:, :], vT_t[_p][:, :], start=(_lt == 0), stop=(_lt == 15)),
                    inc=(b.pe if lt == 15 else None))
            tr_done[g] = kv_mm
        b.wait("vector", b.pe, kv_mm)
        b.op("vector", lambda e: e.tensor_copy(kvt_sb[:, 0:512], psA[4][:, :]))
        kvt_ev = b.op("vector", lambda e: e.tensor_copy(kvt_sb[:, 512:1024], psA[5][:, :]),
                      inc=b.dve)
        if SUB == "kv":
            b.wait("sync", b.dve, kvt_ev)
            b.wait("sync", b.so, b.so.n)
            b.wait("sync", b.sx, b.sx.n)
            block = es.enter_context(nc.Block())
            with es:
                _emit_all(block, b)
            return nc
        b.wait("sync", b.dve, kvt_ev)
        b.dma("sync", b.so, cc_in[:, 0:1024], kvt_sb[:, :])
        pack_done = b.dma("sync", b.so, cc_in[:, 1024:1032], ksum_sb[:, :])
        b.wait("gpsimd", b.so, pack_done)
        ccv = b.op("gpsimd", lambda e: e.collective_compute(
            "AllReduce", ALU.add, replica_groups=[list(range(NCORES))],
            ins=[cc_in.opt()], outs=[cc_out.opt()]), inc=b.cc)
        mm_sweep("q", NH, lambda oc: wq_t[oc],
                 store_chunk=lambda oc: q_s[:, oc * S:(oc + 1) * S])
        b.wait("gpsimd", b.cc, ccv)
        b.dma("gpsimd", b.gq, kvt_sb[:, :], cc_out[:, 0:1024])
        cc_ld = b.dma("gpsimd", b.gq, rn_a[:, :], cc_out[:, 1024:1032])
        b.wait("vector", b.gq, cc_ld)
        rn0 = b.op("vector", lambda e: e.tensor_scalar(rn_b[:, :], rn_a[:, :], EPS_ATT, None, ALU.add),
                   inc=b.dve)
        b.wait("vector", b.dve, rn0)
        rn_done = b.op("vector", lambda e: e.reciprocal(rn_a[:, :], rn_b[:, :]), inc=b.dve)
        b.wait("sync", b.dve, rn_done)
        rn_ld = None
        for g in range(KC):
            rn_ld = b.dma("sync", b.sx, rows[0:1, g * 128:(g + 1) * 128], rn_a[:, g:g + 1])
        b.wait("tensor", b.sx, rn_ld)
        b.wait("tensor", b.dve, b.dve.n)
        b.op("tensor", lambda e: e.matmul(psA[0][:, :], ones_r[:, :], rows[0:1, 0:512],
                                          start=True, stop=True))
        rbc = b.op("tensor", lambda e: e.matmul(psA[1][:, :], ones_r[:, :], rows[0:1, 512:1024],
                                                start=True, stop=True), inc=b.pe)
        b.wait("vector", b.pe, rbc)
        b.op("vector", lambda e: e.tensor_mul(kvtn[:, 0:512], kvt_sb[:, 0:512], psA[0][:, :]))
        kvtn_done = b.op("vector", lambda e: e.tensor_mul(kvtn[:, 512:1024],
                                                          kvt_sb[:, 512:1024], psA[1][:, :]),
                         inc=b.dve)

        if SUB == "cc":
            b.wait("sync", b.dve, kvtn_done)
            b.wait("sync", b.so, b.so.n)
            b.wait("sync", b.sx, b.sx.n)
            block = es.enter_context(nc.Block())
            with es:
                _emit_all(block, b)
            return nc
        b.wait("sync", b.so, b.so.n)
        b.wait("sync", b.pe, b.pe.n)
        qld = {}
        for g4 in range(2):
            qld[g4] = b.dma("sync", b.sx, h_buf[:, g4 * 4 * S:(g4 + 1) * 4 * S],
                            q_s[:, g4 * 4 * S:(g4 + 1) * 4 * S])
        head_pe = {}
        head_act = {}
        for h in range(NH):
            g = h // 4
            base = 4 * (h % 2)
            b.wait("tensor", b.sx, qld[min(h // 4 + 1, NH // 4 - 1)])
            if h == 0:
                b.wait("tensor", b.dve, kvtn_done)
            if h >= 2:
                b.wait("tensor", b.act, head_act[h - 2])
            mm = None
            for tt in range(NT):
                mm = b.op("tensor", lambda e, _h=h, _g=g, _t=tt, _b=base: e.matmul(
                    psA[_b + _t][:, :], kvtn[:, _g * 128:(_g + 1) * 128],
                    h_buf[:, _h * S + _t * TW:_h * S + (_t + 1) * TW],
                    start=True, stop=True),
                    inc=(b.pe if tt == NT - 1 else None))
            head_pe[h] = mm
            if h % 4 == 3 and h // 4 + 2 < NH // 4:
                g4 = h // 4 + 2
                b.wait("sync", b.pe, head_pe[h])
                qld[g4] = b.dma("sync", b.sx, h_buf[:, g4 * 4 * S:(g4 + 1) * 4 * S],
                                q_s[:, g4 * 4 * S:(g4 + 1) * 4 * S])
            b.wait("scalar", b.pe, head_pe[h])
            aa = None
            for tt in range(NT):
                aa = b.op("scalar", lambda e, _h=h, _t=tt, _b=base: e.copy(
                    h_buf[:, _h * S + _t * TW:_h * S + (_t + 1) * TW], psA[_b + _t][:, :]),
                    inc=(b.act if tt == NT - 1 else None))
            head_act[h] = aa

        if SUB == "attn":
            b.wait("sync", b.act, b.act.n)
            b.wait("sync", b.so, b.so.n)
            b.wait("sync", b.sx, b.sx.n)
            block = es.enter_context(nc.Block())
            with es:
                _emit_all(block, b)
            return nc
        d2 = out_tile if STAGE == 2 else (lambda oc, tt: x2_s[oc, :, tt * TW:(tt + 1) * TW])
        mm_sweep("ffn", DC, lambda oc: wo_t[oc], dst_tile=d2,
                 res_tile=lambda oc, tt: x1_s[oc, :, tt * TW:(tt + 1) * TW],
                 bias=bo_sb, scale=None, lora=False, res_ready=x1_store_total)

    if STAGE >= 3:
        x2_store_total = b.so.n
        b.wait("gpsimd", b.so, x2_store_total)
        ln_phase(x2_full)
        lora_u_phase()
        mm_sweep("ffn", DC, lambda oc: wf_t[oc], dst_tile=out_tile,
                 res_tile=lambda oc, tt: x2_s[oc, :, tt * TW:(tt + 1) * TW],
                 bias=bf_sb, scale=0.5, lora=True, res_ready=x2_store_total)

    b.wait("sync", b.so, b.so.n)
    b.wait("sync", b.sx, b.sx.n)

    block = es.enter_context(nc.Block())
    with es:
        _emit_all(block, b)
    return nc


def kernel(**inputs):
    x = np.asarray(inputs["L0_attn_hidden_states"], dtype=np.float32)[0]
    Wq = np.asarray(inputs["L0_attn_Wq"], dtype=np.float32)
    bq = np.asarray(inputs["L0_attn_bq"], dtype=np.float32)
    Wk = np.asarray(inputs["L0_attn_Wk"], dtype=np.float32)
    bk = np.asarray(inputs["L0_attn_bk"], dtype=np.float32)
    Wv = np.asarray(inputs["L0_attn_Wv"], dtype=np.float32)
    bv = np.asarray(inputs["L0_attn_bv"], dtype=np.float32)
    Wo = np.asarray(inputs["L0_attn_Wo"], dtype=np.float32)
    bo = np.asarray(inputs["L0_attn_bo"], dtype=np.float32)
    Wf = np.asarray(inputs["L0_ffn_weight"], dtype=np.float32)
    bf_ = np.asarray(inputs["L0_ffn_bias"], dtype=np.float32)
    lA = np.asarray(inputs["L0_ffn_lora_A"], dtype=np.float32)
    lB = np.asarray(inputs["L0_ffn_lora_B"], dtype=np.float32)
    lsi = np.asarray(inputs["L0_ffn_lora_start_indices"]).astype(np.int32).reshape(1, 4)

    def tile4(W, noc):
        return np.ascontiguousarray(W.reshape(noc, 128, DC, 128).transpose(0, 3, 2, 1))

    def brow(v_, nch):
        return np.ascontiguousarray(v_.reshape(nch, 128).T)

    common = {
        "wq_t": tile4(Wq, DC), "wo_t": tile4(Wo, DC), "wf_t": tile4(Wf, DC),
        "wk_t": tile4(Wk, KC), "wv_t": tile4(Wv, KC),
        "bq_h": brow(bq, DC), "bo_h": brow(bo, DC), "bf_h": brow(bf_, DC),
        "bk_h": brow(bk, KC), "bv_h": brow(bv, KC),
        "la_h": np.ascontiguousarray(lA.reshape(3, RK, DC, 128).transpose(3, 0, 2, 1)),
        "lb_h": np.ascontiguousarray(lB.reshape(3, DC, 128, RK).transpose(3, 0, 1, 2)),
        "lsi_h": lsi,
        "iota_h": np.arange(3, dtype=np.float32).reshape(1, 3),
        "ident_h": np.eye(128, dtype=np.float32),
    }
    xTfull = np.ascontiguousarray(x.T)
    in_maps = []
    for c in range(NCORES):
        m = dict(common)
        m["xT"] = np.ascontiguousarray(xTfull[:, c * S:(c + 1) * S])
        in_maps.append(m)

    nc = build_nc()
    res = run_bass_kernel_spmd(nc, in_maps, core_ids=list(range(NCORES)))
    full = np.concatenate([res.results[c]["out"] for c in range(NCORES)], axis=1)
    return np.ascontiguousarray(full.T).reshape(1, 16384, D).astype(np.float32)


# version 08 (final)
# speedup vs baseline: 1.0240x; 1.0150x over previous
import os
import contextlib
import numpy as np

import concourse.bass as bass
import concourse.mybir as mybir
from concourse.bass_utils import run_bass_kernel_spmd

F32 = mybir.dt.float32
BF16 = mybir.dt.bfloat16
I32 = mybir.dt.int32
AF = mybir.ActivationFunctionType
ALU = mybir.AluOpType
AXX = mybir.AxisListType.X

NCORES = 8
D = 4096
DC = 32
S = 2048
NT = 4
TW = 512
KD = 1024
KC = 8
NH = 32
RK = 16
EPS_LN = 1e-5
EPS_ATT = 1e-10

STAGE = int(os.environ.get("ATHENA_STAGE", "3"))
SUB = os.environ.get("ATHENA_SUB", "full")

ENGINES = ("tensor", "vector", "scalar", "sync", "gpsimd")


class Sem:

    def __init__(self, es, nc, name):
        self.h = es.enter_context(nc.semaphore(name))
        self.name = name
        self.n = 0
        self.bounds = {0}

    def inc(self, by):
        assert not getattr(self, "need_mark", False), \
            (self.name, "dma issued after an unboundaried wait resolved to total", self.n)
        self.n += by
        return self.n

    def mark(self):
        self.bounds.add(self.n)
        self.need_mark = False

    def wait_val(self, v):
        cands = [x for x in self.bounds if x >= v]
        if not cands and self.n >= v:
            self.need_mark = True
            return self.n
        assert cands, ("no boundary for wait", self.name, v, self.n)
        return min(cands)


class B:
    def __init__(self, nc, es):
        self.nc = nc
        self.ops = {e: [] for e in ENGINES}
        self.pe = Sem(es, nc, "pe")
        self.dve = Sem(es, nc, "dve")
        self.act = Sem(es, nc, "act")
        self.cc = Sem(es, nc, "cc")
        self.w0 = Sem(es, nc, "w0")
        self.w1 = Sem(es, nc, "w1")
        self.gq = Sem(es, nc, "gq")
        self.sx = Sem(es, nc, "sx")
        self.so = Sem(es, nc, "so")
        self.lq = [Sem(es, nc, f"lq{i}") for i in range(8)]
        self._dma = {"gpsimd": [self.w0, self.w1, self.gq] + self.lq,
                     "sync": [self.sx, self.so]}
        self._is_dma = {self.w0, self.w1, self.gq, self.sx, self.so, *self.lq}

    def wait(self, eng, sem, v):
        val = sem.wait_val(v) if sem in self._is_dma else v
        assert val <= sem.n, (eng, sem.name, v, sem.n)
        self.ops[eng].append(lambda e, _s=sem.h, _v=val: e.wait_ge(_s, _v))
        for s in self._dma.get(eng, ()):
            s.mark()

    def dma(self, eng, sem, out_ap, in_ap):
        self.ops[eng].append(
            lambda e, _o=out_ap, _i=in_ap, _s=sem.h: e.dma_start(out=_o, in_=_i).then_inc(_s, 16))
        return sem.inc(16)

    def op(self, eng, fn, inc=None):
        if inc is None:
            self.ops[eng].append(fn)
            return None
        self.ops[eng].append(lambda e, _f=fn, _s=inc.h: _f(e).then_inc(_s, 1))
        v = inc.inc(1)
        inc.mark()
        return v


def _emit_all(block, b):
    @block.tensor
    def _(e):
        for f in b.ops["tensor"]:
            f(e)

    @block.vector
    def _(e):
        for f in b.ops["vector"]:
            f(e)

    @block.scalar
    def _(e):
        for f in b.ops["scalar"]:
            f(e)

    @block.sync
    def _(e):
        for f in b.ops["sync"]:
            f(e)

    @block.gpsimd
    def _(e):
        for f in b.ops["gpsimd"]:
            f(e)


def build_nc():
    nc = bass.Bass("TRN2", target_bir_lowering=False)
    es = contextlib.ExitStack()
    ap = lambda t: t.ap()

    xT = ap(nc.dram_tensor("xT", [D, S], F32, kind="ExternalInput"))
    wq_t = ap(nc.dram_tensor("wq_t", [DC, 128, DC, 128], F32, kind="ExternalInput"))
    wo_t = ap(nc.dram_tensor("wo_t", [DC, 128, DC, 128], F32, kind="ExternalInput"))
    wf_t = ap(nc.dram_tensor("wf_t", [DC, 128, DC, 128], F32, kind="ExternalInput"))
    wk_t = ap(nc.dram_tensor("wk_t", [KC, 128, DC, 128], F32, kind="ExternalInput"))
    wv_t = ap(nc.dram_tensor("wv_t", [KC, 128, DC, 128], F32, kind="ExternalInput"))
    bq_h = ap(nc.dram_tensor("bq_h", [128, DC], F32, kind="ExternalInput"))
    bo_h = ap(nc.dram_tensor("bo_h", [128, DC], F32, kind="ExternalInput"))
    bf_h = ap(nc.dram_tensor("bf_h", [128, DC], F32, kind="ExternalInput"))
    bk_h = ap(nc.dram_tensor("bk_h", [128, KC], F32, kind="ExternalInput"))
    bv_h = ap(nc.dram_tensor("bv_h", [128, KC], F32, kind="ExternalInput"))
    la_h = ap(nc.dram_tensor("la_h", [128, 3, DC, RK], F32, kind="ExternalInput"))
    lb_h = ap(nc.dram_tensor("lb_h", [RK, 3, DC, 128], F32, kind="ExternalInput"))
    lsi_h = ap(nc.dram_tensor("lsi_h", [1, 4], I32, kind="ExternalInput"))
    iota_h = ap(nc.dram_tensor("iota_h", [1, 3], F32, kind="ExternalInput"))
    ident_h = ap(nc.dram_tensor("ident_h", [128, 128], F32, kind="ExternalInput"))

    out = ap(nc.dram_tensor("out", [D, S], F32, kind="ExternalOutput"))

    x1_s = ap(nc.dram_tensor("x1_s", [DC, 128, S], F32))
    x2_s = ap(nc.dram_tensor("x2_s", [DC, 128, S], F32))
    q_s = ap(nc.dram_tensor("q_s", [128, NH * S], BF16))
    k_s = ap(nc.dram_tensor("k_s", [KC, 128, S], BF16))
    v_s = ap(nc.dram_tensor("v_s", [KC, 128, S], BF16))
    cc_in = ap(nc.dram_tensor("cc_in", [128, 1032], F32))
    cc_out = ap(nc.dram_tensor("cc_out", [128, 1032], F32))

    sb = lambda n, shp, dt: es.enter_context(nc.sbuf_tensor(n, shp, dt))
    h_buf = sb("h_buf", [128, DC * S], BF16)
    wslab = [sb(f"wslab{i}", [128, DC * 128], BF16) for i in range(2)]
    a_bc = sb("a_bc", [128, S], BF16)
    nm_bc = sb("nm_bc", [128, S], BF16)
    rows = sb("rows", [1, 2048], F32)
    qkvst = [sb(f"qkvst{i}", [128, S], BF16) for i in range(2)]
    u_sb = sb("u_sb", [RK, S], BF16)
    a_eff = sb("a_eff", [128, DC * RK], BF16)
    b_eff = sb("b_eff", [RK, D], BF16)
    bq_sb = sb("bq_sb", [128, DC], F32)
    bo_sb = sb("bo_sb", [128, DC], F32)
    bf_sb = sb("bf_sb", [128, DC], F32)
    bfh_sb = sb("bfh_sb", [128, DC], F32)
    bk_sb = sb("bk_sb", [128, KC], F32)
    bv_sb = sb("bv_sb", [128, KC], F32)
    nbq = sb("nbq", [128, DC], F32)
    bq1 = sb("bq1", [128, DC], F32)
    nbk = sb("nbk", [128, KC], F32)
    bk1 = sb("bk1", [128, KC], F32)
    kvt_sb = sb("kvt_sb", [128, KD], F32)
    kvtn = sb("kvtn", [128, KD], BF16)
    ksum_sb = sb("ksum_sb", [128, KC], F32)
    rn_a = sb("rn_a", [128, KC], F32)
    rn_b = sb("rn_b", [128, KC], F32)
    kT_t = [sb(f"kT{i}", [128, 128], BF16) for i in range(2)]
    vT_t = [sb(f"vT{i}", [128, 128], BF16) for i in range(2)]
    ident_sb = sb("ident_sb", [128, 128], BF16)
    ones_c = sb("ones_c", [128, 1], BF16)
    ones_r = sb("ones_r", [1, 128], F32)
    iota_sb = sb("iota_sb", [1, 3], F32)
    lsi_sb = sb("lsi_sb", [1, 4], I32)
    lsw = sb("lsw", [1, 20], F32)
    m_row = sb("m_row", [1, 3], F32)
    m_sb = sb("m_sb", [128, 3], F32)

    es_setup = contextlib.ExitStack()
    la_st = es_setup.enter_context(nc.sbuf_tensor("la_st", [128, 3 * DC * RK], BF16))
    lb_st = es_setup.enter_context(nc.sbuf_tensor("lb_st", [RK, S], BF16))
    tmp_a = es_setup.enter_context(nc.sbuf_tensor("tmp_a", [128, DC * RK], BF16))
    tmp_b = es_setup.enter_context(nc.sbuf_tensor("tmp_b", [RK, S], BF16))

    psA = [es.enter_context(nc.psum_tensor(f"ps{i}", [128, 512], F32)) for i in range(8)]

    b = B(nc, es)

    b.dma("sync", b.sx, bq_sb[:, :], bq_h)
    b.dma("sync", b.sx, bo_sb[:, :], bo_h)
    b.dma("sync", b.sx, bf_sb[:, :], bf_h)
    b.dma("sync", b.sx, bk_sb[:, :], bk_h)
    b.dma("sync", b.sx, bv_sb[:, :], bv_h)
    b.dma("sync", b.sx, lsi_sb[:, :], lsi_h)
    setup_sx = b.dma("sync", b.sx, iota_sb[:, :], iota_h)
    b.dma("gpsimd", b.gq, ident_sb[:, :], ident_h)
    setup_la = b.dma("gpsimd", b.gq,
                     la_st[:, :].rearrange("p (i c r) -> p i c r", i=3, c=DC), la_h)

    b.op("vector", lambda e: e.memset(ones_c[:, :], 1.0))
    b.op("vector", lambda e: e.memset(ones_r[:, :], 1.0))
    b.wait("vector", b.sx, setup_sx)
    b.op("vector", lambda e: e.tensor_scalar(nbq[:, :], bq_sb[:, :], -1.0, None, ALU.mult))
    b.op("vector", lambda e: e.tensor_scalar(bq1[:, :], bq_sb[:, :], 1.0, None, ALU.add))
    b.op("vector", lambda e: e.tensor_scalar(bfh_sb[:, :], bf_sb[:, :], 0.5, None, ALU.mult))
    b.op("vector", lambda e: e.tensor_scalar(nbk[:, :], bk_sb[:, :], -1.0, None, ALU.mult))
    b.op("vector", lambda e: e.tensor_scalar(bk1[:, :], bk_sb[:, :], 1.0, None, ALU.add))
    L = lsw
    _vc = [None]

    def VC(fn):
        if _vc[0] is not None:
            b.wait("vector", b.dve, _vc[0])
        _vc[0] = b.op("vector", fn, inc=b.dve)
        return _vc[0]

    VC(lambda e: e.tensor_scalar(L[0:1, 0:4], lsi_sb[:, :], 0.0, None, ALU.is_le))
    VC(lambda e: e.tensor_scalar(L[0:1, 4:5], L[0:1, 3:4], 3.0, None, ALU.add))
    VC(lambda e: e.tensor_scalar(L[0:1, 5:6], L[0:1, 1:2], 2.0, None, ALU.mult))
    VC(lambda e: e.tensor_scalar(L[0:1, 6:7], L[0:1, 1:2], -1.0, 1.0, ALU.mult, ALU.add))
    VC(lambda e: e.tensor_mul(L[0:1, 7:8], L[0:1, 6:7], L[0:1, 0:1]))
    VC(lambda e: e.tensor_add(L[0:1, 8:9], L[0:1, 5:6], L[0:1, 7:8]))
    VC(lambda e: e.tensor_mul(L[0:1, 9:10], L[0:1, 2:3], L[0:1, 4:5]))
    VC(lambda e: e.tensor_scalar(L[0:1, 10:11], L[0:1, 2:3], -1.0, 1.0, ALU.mult, ALU.add))
    VC(lambda e: e.tensor_mul(L[0:1, 11:12], L[0:1, 10:11], L[0:1, 8:9]))
    VC(lambda e: e.tensor_add(L[0:1, 12:13], L[0:1, 9:10], L[0:1, 11:12]))
    VC(lambda e: e.tensor_scalar(L[0:1, 13:14], L[0:1, 12:13], -1.0, None, ALU.add))
    VC(lambda e: e.tensor_scalar(L[0:1, 14:15], L[0:1, 13:14], 1.0, None, ALU.is_ge))
    VC(lambda e: e.tensor_scalar(L[0:1, 15:16], L[0:1, 13:14], -1.0, 0.0, ALU.add, ALU.max))
    VC(lambda e: e.tensor_scalar(L[0:1, 16:17], L[0:1, 15:16], 2.0, None, ALU.min))
    VC(lambda e: e.tensor_scalar(L[0:1, 17:20], iota_sb[:, :], L[0:1, 16:17], None, ALU.is_equal))
    mask_done = VC(lambda e: e.tensor_scalar(m_row[:, :], L[0:1, 17:20], L[0:1, 14:15], None, ALU.mult))
    b.wait("tensor", b.dve, mask_done)
    mbc = b.op("tensor", lambda e: e.matmul(psA[0][:, 0:3], ones_r[:, :], m_row[:, :],
                                            start=True, stop=True), inc=b.pe)
    b.wait("vector", b.pe, mbc)
    msb_done = b.op("vector", lambda e: e.tensor_copy(m_sb[:, :], psA[0][:, 0:3]), inc=b.dve)
    b.wait("gpsimd", b.dve, msb_done)

    b.wait("vector", b.gq, setup_la)
    b.wait("vector", b.dve, msb_done)
    _a1 = b.op("vector", lambda e: e.tensor_scalar(a_eff[:, :], la_st[:, 0:DC * RK],
                                                   m_sb[:, 0:1], None, ALU.mult), inc=b.dve)
    b.wait("vector", b.dve, _a1)
    _a2 = b.op("vector", lambda e: e.scalar_tensor_tensor(tmp_a[:, :], la_st[:, DC * RK:2 * DC * RK],
                                                          m_sb[:, 1:2], a_eff[:, :], ALU.mult, ALU.add),
               inc=b.dve)
    b.wait("vector", b.dve, _a2)
    aeff_last = b.op("vector",
                     lambda e: e.scalar_tensor_tensor(a_eff[:, :], la_st[:, 2 * DC * RK:3 * DC * RK],
                                                      m_sb[:, 2:3], tmp_a[:, :], ALU.mult, ALU.add),
                     inc=b.dve)
    b.wait("gpsimd", b.dve, aeff_last)
    done = None
    for hh in range(2):
        cs = slice(hh * 2048, (hh + 1) * 2048)
        for i in range(3):
            ld = b.dma("gpsimd", b.gq,
                       lb_st[:, :].rearrange("r (c o) -> r c o", c=16),
                       lb_h[:, i, hh * 16:(hh + 1) * 16, :])
            b.wait("vector", b.gq, ld)
            if i > 0 or hh > 0:
                b.wait("vector", b.dve, done)
            if i == 0:
                done = b.op("vector", lambda e, _c=cs: e.tensor_scalar(
                    b_eff[:, _c], lb_st[:, :], m_sb[0:RK, 0:1], None, ALU.mult), inc=b.dve)
            elif i == 1:
                done = b.op("vector", lambda e, _c=cs: e.scalar_tensor_tensor(
                    tmp_b[:, :], lb_st[:, :], m_sb[0:RK, 1:2], b_eff[:, _c], ALU.mult, ALU.add),
                    inc=b.dve)
            else:
                done = b.op("vector", lambda e, _c=cs: e.scalar_tensor_tensor(
                    b_eff[:, _c], lb_st[:, :], m_sb[0:RK, 2:3], tmp_b[:, :], ALU.mult, ALU.add),
                    inc=b.dve)
            b.wait("gpsimd", b.dve, done)

    setup_gq_total = b.gq.n
    setup_dve_total = b.dve.n
    es_setup.close()
    xres = [es.enter_context(nc.sbuf_tensor(f"xres{i}", [128, TW], F32)) for i in range(2)]
    outst = [es.enter_context(nc.sbuf_tensor(f"outst{i}", [128, TW], F32)) for i in range(2)]
    xsq = [es.enter_context(nc.sbuf_tensor(f"xsq{i}", [128, 1024], BF16)) for i in range(2)]
    tmp_f = es.enter_context(nc.sbuf_tensor("tmp_f", [128, TW], F32))
    tmp_g = es.enter_context(nc.sbuf_tensor("tmp_g", [128, TW], F32))

    def h_slice(dc, half):
        return h_buf[:, dc * S + half * 1024: dc * S + (half + 1) * 1024]

    def ln_phase(src_full):
        pe_entry = b.pe.n
        b.wait("gpsimd", b.pe, pe_entry)
        b.wait("gpsimd", b.act, b.act.n)
        order = [(dc, half) for half in range(2) for dc in range(DC)]
        ld = {}
        for dcp in range(8):
            ld[dcp] = b.dma("gpsimd", b.lq[dcp % 8],
                            h_buf[:, dcp * S:(dcp + 1) * S], src_full(dcp))
        mm_done = {}
        cp_dve = {}
        bsum = (psA[4], psA[6])
        bsq = (psA[5], psA[7])
        for ki, (dc, half) in enumerate(order):
            p = ki % 2
            b.wait("scalar", b.lq[dc % 8], ld[dc])
            if ki >= 2:
                b.wait("scalar", b.pe, mm_done[order[ki - 2]])
            else:
                b.wait("scalar", b.pe, pe_entry)
            sq = b.op("scalar", lambda e, _p=p, _s=h_slice(dc, half): e.activation(
                xsq[_p][:, :], _s, AF.Square), inc=b.act)
            b.wait("tensor", b.lq[dc % 8], ld[dc])
            b.wait("tensor", b.act, sq)
            st, sp_ = (dc == 0), (dc == DC - 1)
            if ki == 0:
                b.wait("tensor", b.dve, b.dve.n)
            elif half == 1 and dc == 0:
                b.wait("tensor", b.dve, cp_dve[0])
            b.op("tensor", lambda e, _st=st, _sp=sp_, _o=bsum[0], _s=h_slice(dc, half): e.matmul(
                _o[0:1, :], ones_c[:, :], _s[:, 0:512], start=_st, stop=_sp))
            b.op("tensor", lambda e, _p=p, _st=st, _sp=sp_, _o=bsq[0]: e.matmul(
                _o[0:1, :], ones_c[:, :], xsq[_p][:, 0:512], start=_st, stop=_sp))
            b.op("tensor", lambda e, _st=st, _sp=sp_, _o=bsum[1], _s=h_slice(dc, half): e.matmul(
                _o[0:1, :], ones_c[:, :], _s[:, 512:1024], start=_st, stop=_sp))
            mm_done[(dc, half)] = b.op("tensor", lambda e, _p=p, _st=st, _sp=sp_, _o=bsq[1]: e.matmul(
                _o[0:1, :], ones_c[:, :], xsq[_p][:, 512:1024], start=_st, stop=_sp), inc=b.pe)
            if half == 0 and dc + 8 < DC:
                b.wait("gpsimd", b.pe, mm_done[(dc, half)])
                ld[dc + 8] = b.dma("gpsimd", b.lq[(dc + 8) % 8],
                                   h_buf[:, (dc + 8) * S:(dc + 9) * S], src_full(dc + 8))
            if dc != DC - 1:
                continue
            stats_pe = b.pe.n
            for j in range(2):
                S_ps, Q_ps = bsum[j], bsq[j]
                q0 = rows[0:1, 0:512]
                q1 = rows[0:1, 512:1024]
                q2 = rows[0:1, 1024:1536]
                q3 = rows[0:1, 1536:2048]
                b.wait("vector", b.pe, stats_pe if j == 0 else b.pe.n)
                c0 = b.op("vector", lambda e, _S=S_ps, _q=q0: e.tensor_scalar(
                    _q, _S[0:1, :], -1.0 / D, None, ALU.mult), inc=b.dve)
                c1 = b.op("vector", lambda e, _Q=Q_ps, _q=q1: e.tensor_scalar(
                    _q, _Q[0:1, :], 1.0 / D, EPS_LN, ALU.mult, ALU.add), inc=b.dve)
                b.wait("vector", b.dve, c1)
                v0 = b.op("vector", lambda e, _q0=q0, _q2=q2: e.tensor_mul(_q2, _q0, _q0),
                          inc=b.dve)
                b.wait("vector", b.dve, v0)
                v1 = b.op("vector", lambda e, _q1=q1, _q2=q2, _q3=q3: e.tensor_sub(_q3, _q1, _q2),
                          inc=b.dve)
                b.wait("scalar", b.dve, v1)
                a0 = b.op("scalar", lambda e, _q2=q2, _q3=q3: e.activation(_q2, _q3, AF.Ln),
                          inc=b.act)
                b.wait("scalar", b.act, a0)
                a1 = b.op("scalar", lambda e, _q1=q1, _q2=q2: e.activation(
                    _q1, _q2, AF.Exp, bias=0.0, scale=-0.5), inc=b.act)
                b.wait("tensor", b.dve, v1)
                b.wait("tensor", b.act, a1)
                b.op("tensor", lambda e, _S=S_ps, _q=q1: e.matmul(
                    _S[:, :], ones_r[:, :], _q, start=True, stop=True))
                bc = b.op("tensor", lambda e, _Q=Q_ps, _q=q0: e.matmul(
                    _Q[:, :], ones_r[:, :], _q, start=True, stop=True), inc=b.pe)
                b.wait("vector", b.pe, bc)
                cls = slice(half * 1024 + j * 512, half * 1024 + (j + 1) * 512)
                b.op("vector", lambda e, _S=S_ps, _c=cls: e.tensor_copy(a_bc[:, _c], _S[:, :]))
                b.op("vector", lambda e, _Q=Q_ps, _c=cls: e.tensor_copy(nm_bc[:, _c], _Q[:, :]),
                     inc=b.dve)
            cp_dve[half] = b.dve.n
            b.wait("vector", b.dve, cp_dve[half])
            nms = nm_bc[:, half * 1024:(half + 1) * 1024]
            for sdc in range(DC):
                b.op("vector", lambda e, _h=h_slice(sdc, half), _n=nms: e.tensor_tensor(
                    _h, _h, _n, ALU.add), inc=(b.dve if sdc == DC - 1 else None))

    def lora_u_phase():
        b.wait("tensor", b.dve, b.dve.n)
        b.wait("tensor", b.act, b.act.n)
        mm = None
        for dc in range(DC):
            for tt in range(NT):
                mm = b.op("tensor", lambda e, _dc=dc, _t=tt, _st=(dc == 0), _sp=(dc == DC - 1):
                          e.matmul(psA[4 + _t][0:RK, :], a_eff[:, _dc * RK:(_dc + 1) * RK],
                                   h_buf[:, _dc * S + _t * TW:_dc * S + (_t + 1) * TW],
                                   start=_st, stop=_sp),
                          inc=(b.pe if (dc == DC - 1 and tt == NT - 1) else None))
        b.wait("vector", b.pe, mm)
        uv = None
        for tt in range(NT):
            uv = b.op("vector", lambda e, _t=tt: e.tensor_copy(
                u_sb[:, _t * TW:(_t + 1) * TW], psA[4 + _t][0:RK, :]), inc=b.dve)
        return uv

    def mm_sweep(kind, n_oc, slab_src, dst_tile=None, res_tile=None, store_chunk=None,
                 bias=None, scale=None, lora=False, res_ready=0):
        slab_done = {}
        wsem = (b.w0, b.w1)
        b.wait("gpsimd", b.pe, b.pe.n)
        for oc in range(min(2, n_oc)):
            slab_done[oc] = b.dma("gpsimd", wsem[oc % 2],
                                  wslab[oc % 2][:, :].rearrange("p (c o) -> p c o", c=DC),
                                  slab_src(oc))
        grp = {}
        rel_dve = {}
        rel_act = {}
        prev_a1 = {}
        prev_a2 = None
        dve_entry = b.dve.n
        xres_ld = {}
        v2_done = {}
        store_done = {}
        if kind == "ffn":
            if res_ready:
                b.wait("sync", b.so, res_ready)
            b.wait("sync", b.pe, b.pe.n)
            b.wait("sync", b.gq, setup_gq_total)
            b.wait("sync", b.dve, setup_dve_total)
            for kk in range(2):
                xres_ld[kk] = b.dma("sync", b.sx, xres[kk % 2][:, :], res_tile(0, kk))
        for oc in range(n_oc):
            base = 4 * (oc % 2)
            b.wait("tensor", wsem[oc % 2], slab_done[oc])
            if oc == 0:
                b.wait("tensor", b.dve, b.dve.n)
                b.wait("tensor", b.act, b.act.n)
                b.wait("vector", b.dve, dve_entry)
            if oc >= 2:
                b.wait("tensor", b.dve, rel_dve[oc - 2])
                if (oc - 2) in rel_act:
                    b.wait("tensor", b.act, rel_act[oc - 2])
            for dc in range(DC):
                for tt in range(NT):
                    sp_ = (dc == DC - 1) and not lora
                    last = (not lora) and (dc == DC - 1) and (tt == NT - 1)
                    grp[oc] = b.op("tensor",
                                   lambda e, _oc=oc, _dc=dc, _t=tt, _st=(dc == 0), _sp=sp_, _b=base:
                                   e.matmul(psA[_b + _t][:, :],
                                            wslab[_oc % 2][:, _dc * 128:(_dc + 1) * 128],
                                            h_buf[:, _dc * S + _t * TW:_dc * S + (_t + 1) * TW],
                                            start=_st, stop=_sp),
                                   inc=(b.pe if last else None))
            if lora:
                for tt in range(NT):
                    grp[oc] = b.op("tensor", lambda e, _oc=oc, _t=tt, _b=base: e.matmul(
                        psA[_b + _t][:, :], b_eff[:, _oc * 128:(_oc + 1) * 128],
                        u_sb[:, _t * TW:(_t + 1) * TW], start=False, stop=True),
                        inc=(b.pe if tt == NT - 1 else None))
            if oc + 2 < n_oc:
                b.wait("gpsimd", b.pe, grp[oc])
                slab_done[oc + 2] = b.dma("gpsimd", wsem[oc % 2],
                                          wslab[oc % 2][:, :].rearrange("p (c o) -> p c o", c=DC),
                                          slab_src(oc + 2))
            if kind == "ffn":
                for tt in range(NT):
                    k = oc * NT + tt
                    p = k % 2
                    b.wait("vector", b.pe, grp[oc])
                    if k >= 1:
                        b.wait("vector", b.so, store_done[k - 1])
                    if scale is not None:
                        r1 = b.op("vector", lambda e, _oc=oc, _t=tt, _b=base:
                                  e.scalar_tensor_tensor(
                                      tmp_f[:, :], psA[_b + _t][:, :], scale,
                                      a_bc[:, _t * TW:(_t + 1) * TW], ALU.mult, ALU.mult),
                                  inc=b.dve)
                    else:
                        r1 = b.op("vector", lambda e, _oc=oc, _t=tt, _b=base: e.tensor_scalar(
                            tmp_f[:, :], psA[_b + _t][:, :], bias[:, _oc:_oc + 1], None,
                            ALU.add), inc=b.dve)
                    b.wait("vector", b.sx, xres_ld[min(k + 1, n_oc * NT - 1)])
                    b.wait("vector", b.dve, r1)
                    if scale is not None:
                        v2_done[k] = b.op("vector", lambda e, _p=p, _oc=oc:
                                          e.scalar_tensor_tensor(
                                              outst[_p][:, :], xres[_p][:, :],
                                              bfh_sb[:, _oc:_oc + 1], tmp_f[:, :],
                                              ALU.add, ALU.add), inc=b.dve)
                    else:
                        v2_done[k] = b.op("vector", lambda e, _p=p: e.tensor_add(
                            outst[_p][:, :], tmp_f[:, :], xres[_p][:, :]), inc=b.dve)
                    rel_dve[oc] = r1
                    b.wait("sync", b.dve, v2_done[k])
                    store_done[k] = b.dma("sync", b.so, dst_tile(oc, tt), outst[p][:, :])
                    kk = k + 2
                    if kk < n_oc * NT:
                        oc2, tt2 = divmod(kk, NT)
                        xres_ld[kk] = b.dma("sync", b.sx, xres[kk % 2][:, :], res_tile(oc2, tt2))
            elif kind in ("q", "k"):
                nb = nbq if kind == "q" else nbk
                b1 = bq1 if kind == "q" else bk1
                stage = qkvst[oc % 2]
                for tt in range(NT):
                    k = oc * NT + tt
                    tp = k % 2
                    b.wait("vector", b.pe, grp[oc])
                    if k >= 2:
                        b.wait("vector", b.act, prev_a1[tp])
                    z0 = b.op("vector", lambda e, _t=tt, _b=base, _tp=tp: e.tensor_tensor(
                        xres[_tp][:, :], psA[_b + _t][:, :],
                        a_bc[:, _t * TW:(_t + 1) * TW], ALU.mult), inc=b.dve)
                    rel_dve[oc] = z0
                    if tt == 0 and oc >= 1:
                        b.wait("vector", b.so, store_done[oc - 1])
                    b.wait("vector", b.dve, z0)
                    r1 = b.op("vector", lambda e, _oc=oc, _b1=b1, _tp=tp:
                              e.tensor_scalar(outst[_tp][:, :], xres[_tp][:, :],
                                              _b1[:, _oc:_oc + 1], None, ALU.add), inc=b.dve)
                    b.wait("scalar", b.dve, z0)
                    if k >= 1:
                        b.wait("scalar", b.act, prev_a2)
                    a1 = b.op("scalar", lambda e, _oc=oc, _nb=nb, _tp=tp: e.activation(
                        tmp_f[:, :], xres[_tp][:, :], AF.Relu,
                        bias=_nb[:, _oc:_oc + 1], scale=-1.0), inc=b.act)
                    b.wait("scalar", b.act, a1)
                    if k >= 1:
                        b.wait("scalar", b.dve, v2_done[k - 1])
                    a2 = b.op("scalar", lambda e: e.activation(
                        tmp_g[:, :], tmp_f[:, :], AF.Exp, bias=0.0, scale=-1.0), inc=b.act)
                    prev_a1[tp] = a1
                    prev_a2 = a2
                    b.wait("vector", b.act, a2)
                    b.wait("vector", b.dve, r1)
                    v2_done[k] = b.op("vector", lambda e, _tp=tp, _st=stage, _t=tt: e.tensor_tensor(
                        _st[:, _t * TW:(_t + 1) * TW], tmp_g[:, :], outst[_tp][:, :], ALU.max),
                        inc=b.dve)
                if kind == "k":
                    b.wait("vector", b.dve, v2_done[oc * NT + NT - 1])
                    b.op("vector", lambda e, _oc=oc, _st=stage: e.reduce_sum(
                        ksum_sb[:, _oc:_oc + 1], _st[:, :], axis=AXX))
                b.wait("sync", b.dve, v2_done[oc * NT + NT - 1])
                store_done[oc] = b.dma("sync", b.so, store_chunk(oc), stage[:, :])
            elif kind == "v":
                vb = bv_sb if n_oc == KC else bq_sb
                stage = qkvst[oc % 2]
                for tt in range(NT):
                    k = oc * NT + tt
                    tp = k % 2
                    b.wait("vector", b.pe, grp[oc])
                    if tt == 0 and oc >= 1:
                        b.wait("vector", b.so, store_done[oc - 1])
                    if k >= 2:
                        b.wait("vector", b.dve, v2_done[k - 2])
                    z0 = b.op("vector", lambda e, _t=tt, _b=base, _tp=tp: e.tensor_tensor(
                        xres[_tp][:, :], psA[_b + _t][:, :],
                        a_bc[:, _t * TW:(_t + 1) * TW], ALU.mult), inc=b.dve)
                    b.wait("vector", b.dve, z0)
                    r1 = b.op("vector", lambda e, _oc=oc, _t=tt, _st=stage, _vb=vb, _tp=tp:
                              e.tensor_scalar(_st[:, _t * TW:(_t + 1) * TW], xres[_tp][:, :],
                                              _vb[:, _oc:_oc + 1], None, ALU.add), inc=b.dve)
                    rel_dve[oc] = z0
                    v2_done[k] = r1
                b.wait("sync", b.dve, v2_done[oc * NT + NT - 1])
                store_done[oc] = b.dma("sync", b.so, store_chunk(oc), stage[:, :])
        return store_done

    x_full = lambda dc: xT[dc * 128:(dc + 1) * 128, :]
    x1_full = lambda dc: x1_s[dc, :, :]
    x2_full = lambda dc: x2_s[dc, :, :]
    out_tile = lambda oc, tt: out[oc * 128:(oc + 1) * 128, tt * TW:(tt + 1) * TW]

    ln_phase(x_full)
    u_ready = lora_u_phase()
    if STAGE == 1:
        d1 = out_tile
    else:
        d1 = lambda oc, tt: x1_s[oc, :, tt * TW:(tt + 1) * TW]
    mm_sweep("ffn", DC, lambda oc: wf_t[oc], dst_tile=d1,
             res_tile=lambda oc, tt: xT[oc * 128:(oc + 1) * 128, tt * TW:(tt + 1) * TW],
             bias=bf_sb, scale=0.5, lora=True)

    if STAGE >= 2:
        x1_store_total = b.so.n
        b.wait("gpsimd", b.so, x1_store_total)
        ln_phase(x1_full)
        if SUB != "ln2":
            mm_sweep("k", KC, lambda oc: wk_t[oc], store_chunk=lambda oc: k_s[oc])
            mm_sweep("v", KC, lambda oc: wv_t[oc], store_chunk=lambda oc: v_s[oc])
        kv_store_total = b.so.n

        if SUB in ("ln2", "qkv"):
            b.wait("sync", b.so, b.so.n)
            b.wait("sync", b.sx, b.sx.n)
            block = es.enter_context(nc.Block())
            with es:
                _emit_all(block, b)
            return nc
        b.wait("gpsimd", b.so, kv_store_total)
        kv_mm = None
        tr_done = {}
        b.wait("tensor", b.dve, b.dve.n)
        b.wait("tensor", b.act, b.act.n)
        last_cv = {}
        last_cvv = {}
        for g in range(KC):
            if g >= 1:
                b.wait("gpsimd", b.pe, tr_done[g - 1])
            b.dma("gpsimd", b.gq, qkvst[0][:, :], k_s[g])
            kvld = b.dma("gpsimd", b.gq, qkvst[1][:, :], v_s[g])
            b.wait("tensor", b.gq, kvld)
            pend = None
            cv_of = {}
            cvv_of = {}
            for lt in range(17):
                if lt < 16:
                    tb = (lt % 2) * 2
                    if tb in last_cv:
                        b.wait("tensor", b.dve, last_cv[tb])
                        b.wait("tensor", b.act, last_cvv[tb])
                    b.op("tensor", lambda e, _lt=lt, _tb=tb: e.transpose(
                        psA[_tb][:, 0:64].bitcast(BF16), qkvst[0][:, _lt * 128:(_lt + 1) * 128],
                        ident_sb[:, :]))
                    t2 = b.op("tensor", lambda e, _lt=lt, _tb=tb: e.transpose(
                        psA[_tb + 1][:, 0:64].bitcast(BF16), qkvst[1][:, _lt * 128:(_lt + 1) * 128],
                        ident_sb[:, :]),
                        inc=b.pe)
                if pend is not None:
                    _plt = pend
                    b.wait("tensor", b.dve, cv_of[_plt])
                    b.wait("tensor", b.act, cvv_of[_plt])
                    kv_mm = b.op("tensor", lambda e, _g=g, _lt=_plt, _p=_plt % 2: e.matmul(
                        psA[4 + (_g // 4)][:, (_g % 4) * 128:(_g % 4 + 1) * 128],
                        kT_t[_p][:, :], vT_t[_p][:, :], start=(_lt == 0), stop=(_lt == 15)),
                        inc=(b.pe if _plt == 15 else None))
                if lt < 16:
                    b.wait("vector", b.pe, t2)
                    cv_of[lt] = b.op("vector", lambda e, _tb=tb, _p=lt % 2: e.tensor_copy(
                        kT_t[_p][:, :], psA[_tb][:, 0:64].bitcast(BF16)), inc=b.dve)
                    b.wait("scalar", b.pe, t2)
                    cvv_of[lt] = b.op("scalar", lambda e, _tb=tb, _p=lt % 2: e.copy(
                        vT_t[_p][:, :], psA[_tb + 1][:, 0:64].bitcast(BF16)), inc=b.act)
                    last_cv[tb] = cv_of[lt]
                    last_cvv[tb] = cvv_of[lt]
                    pend = lt
            tr_done[g] = kv_mm
        b.wait("vector", b.pe, kv_mm)
        b.op("vector", lambda e: e.tensor_copy(kvt_sb[:, 0:512], psA[4][:, :]))
        kvt_ev = b.op("vector", lambda e: e.tensor_copy(kvt_sb[:, 512:1024], psA[5][:, :]),
                      inc=b.dve)
        if SUB == "kv":
            b.wait("sync", b.dve, kvt_ev)
            b.wait("sync", b.so, b.so.n)
            b.wait("sync", b.sx, b.sx.n)
            block = es.enter_context(nc.Block())
            with es:
                _emit_all(block, b)
            return nc
        b.wait("sync", b.dve, kvt_ev)
        b.dma("sync", b.so, cc_in[:, 0:1024], kvt_sb[:, :])
        pack_done = b.dma("sync", b.so, cc_in[:, 1024:1032], ksum_sb[:, :])
        b.wait("gpsimd", b.so, pack_done)
        ccv = b.op("gpsimd", lambda e: e.collective_compute(
            "AllReduce", ALU.add, replica_groups=[list(range(NCORES))],
            ins=[cc_in.opt()], outs=[cc_out.opt()]), inc=b.cc)
        mm_sweep("q", NH, lambda oc: wq_t[oc],
                 store_chunk=lambda oc: q_s[:, oc * S:(oc + 1) * S])
        b.wait("gpsimd", b.cc, ccv)
        b.dma("gpsimd", b.gq, kvt_sb[:, :], cc_out[:, 0:1024])
        cc_ld = b.dma("gpsimd", b.gq, rn_a[:, :], cc_out[:, 1024:1032])
        b.wait("vector", b.gq, cc_ld)
        rn0 = b.op("vector", lambda e: e.tensor_scalar(rn_b[:, :], rn_a[:, :], EPS_ATT, None, ALU.add),
                   inc=b.dve)
        b.wait("vector", b.dve, rn0)
        rn_done = b.op("vector", lambda e: e.reciprocal(rn_a[:, :], rn_b[:, :]), inc=b.dve)
        b.wait("sync", b.dve, rn_done)
        rn_ld = None
        for g in range(KC):
            rn_ld = b.dma("sync", b.sx, rows[0:1, g * 128:(g + 1) * 128], rn_a[:, g:g + 1])
        b.wait("tensor", b.sx, rn_ld)
        b.wait("tensor", b.dve, b.dve.n)
        b.op("tensor", lambda e: e.matmul(psA[0][:, :], ones_r[:, :], rows[0:1, 0:512],
                                          start=True, stop=True))
        rbc = b.op("tensor", lambda e: e.matmul(psA[1][:, :], ones_r[:, :], rows[0:1, 512:1024],
                                                start=True, stop=True), inc=b.pe)
        b.wait("vector", b.pe, rbc)
        b.op("vector", lambda e: e.tensor_mul(kvtn[:, 0:512], kvt_sb[:, 0:512], psA[0][:, :]))
        kvtn_done = b.op("vector", lambda e: e.tensor_mul(kvtn[:, 512:1024],
                                                          kvt_sb[:, 512:1024], psA[1][:, :]),
                         inc=b.dve)

        if SUB == "cc":
            b.wait("sync", b.dve, kvtn_done)
            b.wait("sync", b.so, b.so.n)
            b.wait("sync", b.sx, b.sx.n)
            block = es.enter_context(nc.Block())
            with es:
                _emit_all(block, b)
            return nc
        b.wait("sync", b.so, b.so.n)
        b.wait("sync", b.pe, b.pe.n)
        qld = {}
        for g4 in range(2):
            qld[g4] = b.dma("sync", b.sx, h_buf[:, g4 * 4 * S:(g4 + 1) * 4 * S],
                            q_s[:, g4 * 4 * S:(g4 + 1) * 4 * S])
        head_pe = {}
        head_act = {}
        for h in range(NH):
            g = h // 4
            base = 4 * (h % 2)
            b.wait("tensor", b.sx, qld[min(h // 4 + 1, NH // 4 - 1)])
            if h == 0:
                b.wait("tensor", b.dve, kvtn_done)
            if h >= 2:
                b.wait("tensor", b.act, head_act[h - 2])
            mm = None
            for tt in range(NT):
                mm = b.op("tensor", lambda e, _h=h, _g=g, _t=tt, _b=base: e.matmul(
                    psA[_b + _t][:, :], kvtn[:, _g * 128:(_g + 1) * 128],
                    h_buf[:, _h * S + _t * TW:_h * S + (_t + 1) * TW],
                    start=True, stop=True),
                    inc=(b.pe if tt == NT - 1 else None))
            head_pe[h] = mm
            if h % 4 == 3 and h // 4 + 2 < NH // 4:
                g4 = h // 4 + 2
                b.wait("sync", b.pe, head_pe[h])
                qld[g4] = b.dma("sync", b.sx, h_buf[:, g4 * 4 * S:(g4 + 1) * 4 * S],
                                q_s[:, g4 * 4 * S:(g4 + 1) * 4 * S])
            b.wait("scalar", b.pe, head_pe[h])
            aa = None
            for tt in range(NT):
                aa = b.op("scalar", lambda e, _h=h, _t=tt, _b=base: e.copy(
                    h_buf[:, _h * S + _t * TW:_h * S + (_t + 1) * TW], psA[_b + _t][:, :]),
                    inc=(b.act if tt == NT - 1 else None))
            head_act[h] = aa

        if SUB == "attn":
            b.wait("sync", b.act, b.act.n)
            b.wait("sync", b.so, b.so.n)
            b.wait("sync", b.sx, b.sx.n)
            block = es.enter_context(nc.Block())
            with es:
                _emit_all(block, b)
            return nc
        d2 = out_tile if STAGE == 2 else (lambda oc, tt: x2_s[oc, :, tt * TW:(tt + 1) * TW])
        mm_sweep("ffn", DC, lambda oc: wo_t[oc], dst_tile=d2,
                 res_tile=lambda oc, tt: x1_s[oc, :, tt * TW:(tt + 1) * TW],
                 bias=bo_sb, scale=None, lora=False, res_ready=x1_store_total)

    if STAGE >= 3:
        x2_store_total = b.so.n
        b.wait("gpsimd", b.so, x2_store_total)
        ln_phase(x2_full)
        lora_u_phase()
        mm_sweep("ffn", DC, lambda oc: wf_t[oc], dst_tile=out_tile,
                 res_tile=lambda oc, tt: x2_s[oc, :, tt * TW:(tt + 1) * TW],
                 bias=bf_sb, scale=0.5, lora=True, res_ready=x2_store_total)

    b.wait("sync", b.so, b.so.n)
    b.wait("sync", b.sx, b.sx.n)

    block = es.enter_context(nc.Block())
    with es:
        _emit_all(block, b)
    return nc


def kernel(**inputs):
    x = np.asarray(inputs["L0_attn_hidden_states"], dtype=np.float32)[0]
    Wq = np.asarray(inputs["L0_attn_Wq"], dtype=np.float32)
    bq = np.asarray(inputs["L0_attn_bq"], dtype=np.float32)
    Wk = np.asarray(inputs["L0_attn_Wk"], dtype=np.float32)
    bk = np.asarray(inputs["L0_attn_bk"], dtype=np.float32)
    Wv = np.asarray(inputs["L0_attn_Wv"], dtype=np.float32)
    bv = np.asarray(inputs["L0_attn_bv"], dtype=np.float32)
    Wo = np.asarray(inputs["L0_attn_Wo"], dtype=np.float32)
    bo = np.asarray(inputs["L0_attn_bo"], dtype=np.float32)
    Wf = np.asarray(inputs["L0_ffn_weight"], dtype=np.float32)
    bf_ = np.asarray(inputs["L0_ffn_bias"], dtype=np.float32)
    lA = np.asarray(inputs["L0_ffn_lora_A"], dtype=np.float32)
    lB = np.asarray(inputs["L0_ffn_lora_B"], dtype=np.float32)
    lsi = np.asarray(inputs["L0_ffn_lora_start_indices"]).astype(np.int32).reshape(1, 4)

    def tile4(W, noc):
        return np.ascontiguousarray(W.reshape(noc, 128, DC, 128).transpose(0, 3, 2, 1))

    def brow(v_, nch):
        return np.ascontiguousarray(v_.reshape(nch, 128).T)

    common = {
        "wq_t": tile4(Wq, DC), "wo_t": tile4(Wo, DC), "wf_t": tile4(Wf, DC),
        "wk_t": tile4(Wk, KC), "wv_t": tile4(Wv, KC),
        "bq_h": brow(bq, DC), "bo_h": brow(bo, DC), "bf_h": brow(bf_, DC),
        "bk_h": brow(bk, KC), "bv_h": brow(bv, KC),
        "la_h": np.ascontiguousarray(lA.reshape(3, RK, DC, 128).transpose(3, 0, 2, 1)),
        "lb_h": np.ascontiguousarray(lB.reshape(3, DC, 128, RK).transpose(3, 0, 1, 2)),
        "lsi_h": lsi,
        "iota_h": np.arange(3, dtype=np.float32).reshape(1, 3),
        "ident_h": np.eye(128, dtype=np.float32),
    }
    xTfull = np.ascontiguousarray(x.T)
    in_maps = []
    for c in range(NCORES):
        m = dict(common)
        m["xT"] = np.ascontiguousarray(xTfull[:, c * S:(c + 1) * S])
        in_maps.append(m)

    nc = build_nc()
    res = run_bass_kernel_spmd(nc, in_maps, core_ids=list(range(NCORES)))
    full = np.concatenate([res.results[c]["out"] for c in range(NCORES)], axis=1)
    return np.ascontiguousarray(full.T).reshape(1, 16384, D).astype(np.float32)
